# version 03
# baseline (speedup 1.0000x reference)
import numpy as np
import ml_dtypes
from contextlib import ExitStack
import concourse.bass as bass
import concourse.mybir as mybir
from concourse.bass_utils import run_bass_kernel_spmd

F32 = mybir.dt.float32
BF16 = mybir.dt.bfloat16
AF = mybir.ActivationFunctionType
ALU = mybir.AluOpType
AX = mybir.AxisListType

D = 1024
T = 4096
TQ = 2048
NH = 16
HD = 64
NP = 8
NE = 8
FF = 4096
DC = D // 128
FC = FF // 128
TT_N = T // 128
TQ_N = TQ // 128
WIN = 1024
NW = TQ // WIN


class Tk:

    def __init__(self, nc):
        self.nc = nc
        self.e = {}
        self.res = {}

    def add_engine(self, name, obj, sem, step):
        self.e[name] = {"o": obj, "sem": sem, "step": step, "cnt": 0, "waited": {}}

    def _need(self, deps, tok):
        if tok is None:
            return
        en, v = tok
        if deps.get(en, 0) < v:
            deps[en] = v

    def _bufsem(self, key, issuer):
        nm = f"D:{key}"
        if nm not in self.e:
            sem = self.nc.semaphore(f"ds_{key}").__enter__()
            self.e[nm] = {"o": None, "sem": sem, "step": 16, "cnt": 0, "waited": {}}
        return nm

    def op(self, ename, emit, reads=(), writes=(), sem_key=None):
        e = self.e[ename]
        is_dma = ename in ("d", "g")
        deps = {}
        for r in reads:
            rr = self.res.get(r)
            if rr:
                self._need(deps, rr["w"])
        for w in writes:
            rr = self.res.get(w)
            if rr:
                self._need(deps, rr["w"])
                for t in rr["r"]:
                    self._need(deps, t)
        for pn, v in sorted(deps.items()):
            if pn == ename and ename == "T":
                continue
            if e["waited"].get(pn, 0) >= v:
                continue
            pe = self.e[pn]
            assert v <= pe["cnt"], (ename, pn, v, pe["cnt"])
            e["o"].wait_ge(pe["sem"], v)
            e["waited"][pn] = v
        inst = emit()
        if is_dma:
            snm = self._bufsem(sem_key if sem_key is not None else writes[0], ename)
            se = self.e[snm]
            inst.then_inc(se["sem"], 16)
            se["cnt"] += 16
            tok = (snm, se["cnt"])
            e["waited"][snm] = 0
        else:
            inst.then_inc(e["sem"], e["step"])
            e["cnt"] += e["step"]
            tok = (ename, e["cnt"])
        for w in writes:
            self.res[w] = {"w": tok, "r": []}
        for r in reads:
            rr = self.res.setdefault(r, {"w": None, "r": []})
            rr["r"].append(tok)
        return inst

    def barrier(self):
        for en, e in self.e.items():
            if e["o"] is None:
                continue
            for pn, pe in self.e.items():
                if pn == en or pe["cnt"] == 0:
                    continue
                if e["waited"].get(pn, 0) < pe["cnt"]:
                    e["o"].wait_ge(pe["sem"], pe["cnt"])
                    e["waited"][pn] = pe["cnt"]

    def final_waits(self, ename):
        e = self.e[ename]
        for pn, pe in self.e.items():
            if pn == ename or pe["cnt"] == 0 or pe["o"] is e["o"]:
                continue
            if e["waited"].get(pn, 0) < pe["cnt"]:
                e["o"].wait_ge(pe["sem"], pe["cnt"])
                e["waited"][pn] = pe["cnt"]


def build():
    nc = bass.Bass("TRN2", target_bir_lowering=False)
    dt = lambda n, s, d=F32, k="ExternalInput": nc.dram_tensor(n, s, d, kind=k)

    xb = dt("xb", [T, D])
    wq = dt("wq", [D, D]); wqr_d = dt("wqr", [D, D])
    wk = dt("wk", [D, D]); wkr_d = dt("wkr", [D, D])
    wv = dt("wv", [D, D]); wo = dt("wo", [D, D])
    bqkT_d = dt("bqkTd", [4 * DC, 128])
    bv_d = dt("bv", [1, D]); bo_d = dt("bo", [1, D])
    gate = dt("gate", [D, NE])
    w1r = dt("w1r", [NE, FC, D, 128])
    w2r = dt("w2r", [NE, DC, FF, 128])
    b1 = dt("b1", [NE, FC, 128])
    b2 = dt("b2", [NE, D])
    cosk = dt("cosk", [128, T]); sink = dt("sink", [128, T])
    ident = dt("ident", [128, 128])
    identb = dt("identb", [128, 128], BF16)
    out_c = dt("out_c", [TQ, D], F32, k="ExternalOutput")

    tk = Tk(nc)
    st = ExitStack()
    sem = lambda n: st.enter_context(nc.semaphore(n))
    ct, cv, cs, cg = sem("ct"), sem("cv"), sem("cs"), sem("cg")
    sd, sg = sem("sd"), sem("sg")
    tk.add_engine("T", nc.tensor, ct, 1)
    tk.add_engine("V", nc.vector, cv, 1)
    tk.add_engine("S", nc.scalar, cs, 1)
    tk.add_engine("G", nc.gpsimd, cg, 1)
    tk.add_engine("d", nc.sync, sd, 16)
    tk.add_engine("g", nc.gpsimd, sg, 16)

    sbuf = lambda es, n, s, d=F32: es.enter_context(nc.sbuf_tensor(n, s, d))

    xnT_o = [sbuf(st, f"xnT_o{i}", [128, TQ], BF16) for i in range(DC)]
    idf = sbuf(st, "idf", [128, 128])
    idb = sbuf(st, "idb", [128, 128], BF16)
    ones1 = sbuf(st, "ones1", [1, 128])
    onesc = sbuf(st, "onesc", [128, 1])
    gate_s = sbuf(st, "gate_s", [128, DC * NE])
    comb = sbuf(st, "comb", [1, NE])
    cbc = sbuf(st, "cbc", [128, NE])
    b2bcT = sbuf(st, "b2bcT", [128, DC])
    bqkT = sbuf(st, "bqkT", [128, 4 * DC])
    logit = sbuf(st, "logit", [1, NE])
    rsm = sbuf(st, "rsm", [1, 8])
    rmask = sbuf(st, "rmask", [1, NE])
    rl2 = sbuf(st, "rl2", [1, NE])
    rexp = sbuf(st, "rexp", [1, NE])
    seqT = sbuf(st, "seqT", [128, DC])
    b1T = sbuf(st, "b1T", [128, FC])

    def mkbank(tag, elsize=4):
        per = 2048 // elsize
        return lambda lo, hi: [f"{tag}{i}" for i in range(lo // per, (hi + per - 1) // per)]

    tk.op("d", lambda: nc.sync.dma_start(idf[:, :], ident[:, :]), writes=["idf"])
    tk.op("d", lambda: nc.sync.dma_start(idb[:, :], identb[:, :]), writes=["idb"])
    for h in range(DC):
        tk.op("g", lambda h=h: nc.gpsimd.dma_start(
            gate_s[:, h * NE:(h + 1) * NE], gate[h * 128:(h + 1) * 128, :],
            allow_slow_non_contiguous=True), writes=["gate_s"])
    tk.op("G", lambda: nc.gpsimd.memset(ones1[:, :], 1.0), writes=["ones1"])
    tk.op("G", lambda: nc.gpsimd.memset(onesc[:, :], 1.0), writes=["onesc"])

    es2 = ExitStack()
    y_acc = [sbuf(es2, f"y{i}", [128, D], BF16) for i in range(TQ_N)]
    es1 = ExitStack()
    xnT_x = [sbuf(es1, f"xnT_x{i}", [128, TQ], BF16) for i in range(DC)]

    with ExitStack() as p1:
        S = p1.enter_context(nc.psum_tensor("p1S", [128, 1024], F32))
        Y = p1.enter_context(nc.psum_tensor("p1Y", [128, 2048], F32))
        sbank = mkbank("p1S"); ybank = mkbank("p1Y")
        xt0 = sbuf(p1, "xt0", [128, D]); xt1 = sbuf(p1, "xt1", [128, D])
        xnt = sbuf(p1, "xnt", [128, D]); xsum = sbuf(p1, "xsum", [128, D])
        stat = sbuf(p1, "stat", [128, 12]); mv = sbuf(p1, "mv", [128, 2])
        rstd = sbuf(p1, "rstd", [128, 1]); nmr = sbuf(p1, "nmr", [128, 1])
        seqrow = sbuf(p1, "seqrow", [1, D])
        cT = sbuf(p1, "cT", [NE, 1]); b2_s = sbuf(p1, "b2_s", [NE, D])
        b2bc = sbuf(p1, "b2bc", [1, D]); lgT = sbuf(p1, "lgT", [NE, 1])
        braw = sbuf(p1, "braw", [4 * DC, 128])

        tk.op("d", lambda: nc.sync.dma_start(braw[:, :], bqkT_d[:, :]), writes=["braw"])
        tk.op("T", lambda: nc.tensor.transpose(Y[:, 0:4 * DC], braw[:, :], idf[0:4 * DC, 0:4 * DC]),
              reads=["braw", "idf"], writes=ybank(0, 32))
        tk.op("V", lambda: nc.vector.tensor_copy(bqkT[:, :], Y[:, 0:4 * DC]),
              reads=ybank(0, 32), writes=["bqkT"])

        xts = [xt0, xt1]
        for tt in range(TT_N):
            xt = xts[tt % 2]; xnm = f"xt{tt % 2}"
            tk.op("d", lambda tt=tt, xt=xt: nc.sync.dma_start(
                xt[:, :], xb[tt * 128:(tt + 1) * 128, :]), writes=[xnm])
            tk.op("V", lambda xt=xt: nc.vector.bn_stats(stat[:, 0:6], xt[:, 0:512]),
                  reads=[xnm], writes=["stat"])
            tk.op("V", lambda xt=xt: nc.vector.bn_stats(stat[:, 6:12], xt[:, 512:1024]),
                  reads=[xnm], writes=["stat2"])
            tk.op("V", lambda: nc.vector.bn_aggr(mv[:, :], stat[:, :]),
                  reads=["stat", "stat2"], writes=["mv"])
            tk.op("V", lambda: nc.vector.tensor_scalar_add(rstd[:, :], mv[:, 1:2], 1e-5),
                  reads=["mv"], writes=["rstd"])
            tk.op("S", lambda: nc.scalar.activation(rstd[:, :], rstd[:, :], AF.Sqrt),
                  reads=["rstd"], writes=["rstd"])
            tk.op("V", lambda: nc.vector.reciprocal(rstd[:, :], rstd[:, :]),
                  reads=["rstd"], writes=["rstd"])
            tk.op("V", lambda: nc.vector.tensor_tensor(nmr[:, :], mv[:, 0:1], rstd[:, :], ALU.mult),
                  reads=["mv", "rstd"], writes=["nmr"])
            tk.op("V", lambda: nc.vector.tensor_scalar_mul(nmr[:, :], nmr[:, :], -1.0),
                  reads=["nmr"], writes=["nmr"])
            tk.op("S", lambda xt=xt: nc.scalar.activation(
                xnt[:, :], xt[:, :], AF.Identity, bias=nmr[:, :], scale=rstd[:, :]),
                reads=[xnm, "nmr", "rstd"], writes=["xnt"])
            if tt == 0:
                tk.op("V", lambda: nc.vector.tensor_copy(xsum[:, :], xnt[:, :]),
                      reads=["xnt"], writes=["xsum"])
            else:
                tk.op("V", lambda: nc.vector.tensor_tensor(xsum[:, :], xsum[:, :], xnt[:, :], ALU.add),
                      reads=["xnt", "xsum"], writes=["xsum"])

            def emit_tr():
                for h in range(DC):
                    i = nc.tensor.transpose(S[:, h * 128:(h + 1) * 128],
                                            xnt[:, h * 128:(h + 1) * 128], idf[:, :])
                return i
            tk.op("T", emit_tr, reads=["xnt", "idf"], writes=sbank(0, 1024))
            own = tt < TQ_N
            dsts = xnT_o if own else xnT_x
            col = (tt if own else tt - TQ_N) * 128

            def emit_cp(dsts=dsts, col=col):
                for h in range(DC):
                    i = nc.vector.tensor_copy(dsts[h][:, col:col + 128],
                                              S[:, h * 128:(h + 1) * 128])
                return i
            tk.op("V", emit_cp, reads=sbank(0, 1024),
                  writes=[("xnT_o" if own else "xnT_x") + str(h) for h in range(DC)])

        def emit_seq():
            for c in range(2):
                i = nc.tensor.matmul(S[0:1, c * 512:(c + 1) * 512], onesc[:, :],
                                     xsum[:, c * 512:(c + 1) * 512], start=True, stop=True)
            return i
        tk.op("T", emit_seq, reads=["onesc", "xsum"], writes=sbank(0, 1024))
        tk.op("S", lambda: nc.scalar.copy(seqrow[:, :], S[0:1, 0:1024]),
              reads=sbank(0, 1024), writes=["seqrow"])

        def emit_seqT():
            for h in range(DC):
                i = nc.tensor.transpose(Y[:, h:h + 1], seqrow[0:1, h * 128:(h + 1) * 128],
                                        idf[0:1, 0:1])
            return i
        tk.op("T", emit_seqT, reads=["seqrow", "idf"], writes=ybank(0, 8))
        tk.op("V", lambda: nc.vector.tensor_copy(seqT[:, :], Y[:, 0:DC]),
              reads=ybank(0, 8), writes=["seqT"])

        def emit_log():
            for h in range(DC):
                i = nc.tensor.matmul(Y[0:NE, 512:513], gate_s[:, h * NE:(h + 1) * NE],
                                     seqT[:, h:h + 1], start=(h == 0), stop=(h == DC - 1))
            return i
        tk.op("T", emit_log, reads=["gate_s", "seqT"], writes=ybank(512, 513))
        tk.op("S", lambda: nc.scalar.activation(lgT[:, :], Y[0:NE, 512:513],
                                                AF.Identity, scale=1.0 / T),
              reads=ybank(512, 513), writes=["lgT"])
        tk.op("T", lambda: nc.tensor.transpose(Y[0:1, 0:NE], lgT[:, :], idf[0:NE, 0:NE]),
              reads=["lgT", "idf"], writes=ybank(0, 8))
        tk.op("V", lambda: nc.vector.tensor_copy(logit[:, :], Y[0:1, 0:NE]),
              reads=ybank(0, 8), writes=["logit"])
        tk.op("V", lambda: nc.vector.reduce_max(rsm[:, 0:1], logit[:, :], axis=AX.X),
              reads=["logit"], writes=["rm1"])
        tk.op("V", lambda: nc.vector.tensor_scalar(rmask[:, :], logit[:, :], rsm[:, 0:1],
                                                   None, op0=ALU.is_ge),
              reads=["logit", "rm1"], writes=["rmask"])
        tk.op("V", lambda: nc.vector.scalar_tensor_tensor(rl2[:, :], rmask[:, :], -1e9,
                                                          logit[:, :], ALU.mult, ALU.add),
              reads=["rmask", "logit"], writes=["rl2"])
        tk.op("V", lambda: nc.vector.reduce_max(rsm[:, 2:3], rl2[:, :], axis=AX.X),
              reads=["rl2"], writes=["rm2"])
        tk.op("V", lambda: nc.vector.tensor_scalar(rl2[:, :], rl2[:, :], rsm[:, 2:3],
                                                   None, op0=ALU.is_ge),
              reads=["rl2", "rm2"], writes=["rl2"])
        tk.op("V", lambda: nc.vector.tensor_tensor(rmask[:, :], rmask[:, :], rl2[:, :], ALU.add),
              reads=["rmask", "rl2"], writes=["rmask"])
        tk.op("V", lambda: nc.vector.tensor_scalar_mul(rsm[:, 1:2], rsm[:, 0:1], -1.0),
              reads=["rm1"], writes=["rm1n"])
        tk.op("S", lambda: nc.scalar.activation(rexp[:, :], logit[:, :], AF.Exp, bias=rsm[:, 1:2]),
              reads=["logit", "rm1n"], writes=["rexp"])
        tk.op("V", lambda: nc.vector.tensor_tensor(rexp[:, :], rexp[:, :], rmask[:, :], ALU.mult),
              reads=["rexp", "rmask"], writes=["rexp"])
        tk.op("V", lambda: nc.vector.reduce_sum(rsm[:, 3:4], rexp[:, :], axis=AX.X),
              reads=["rexp"], writes=["rden"])
        tk.op("V", lambda: nc.vector.reciprocal(rsm[:, 4:5], rsm[:, 3:4]),
              reads=["rden"], writes=["rrec"])
        tk.op("V", lambda: nc.vector.tensor_scalar(comb[:, :], rexp[:, :], rsm[:, 4:5],
                                                   None, op0=ALU.mult),
              reads=["rexp", "rrec"], writes=["comb"])
        tk.op("T", lambda: nc.tensor.matmul(Y[:, 0:NE], ones1[:, :], comb[:, :],
                                            start=True, stop=True),
              reads=["ones1", "comb"], writes=ybank(0, 8))
        tk.op("V", lambda: nc.vector.tensor_copy(cbc[:, :], Y[:, 0:NE]),
              reads=ybank(0, 8), writes=["cbc"])
        tk.op("T", lambda: nc.tensor.transpose(Y[0:NE, 512:513], comb[0:1, :], idf[0:1, 0:1]),
              reads=["comb", "idf"], writes=ybank(512, 513))
        tk.op("V", lambda: nc.vector.tensor_copy(cT[:, :], Y[0:NE, 512:513]),
              reads=ybank(512, 513), writes=["cT"])
        tk.op("d", lambda: nc.sync.dma_start(b2_s[:, :], b2[:, :]), writes=["b2_s"])

        def emit_b2bc():
            for c in range(2):
                i = nc.tensor.matmul(Y[0:1, 1024 + c * 512:1024 + (c + 1) * 512], cT[:, :],
                                     b2_s[:, c * 512:(c + 1) * 512], start=True, stop=True)
            return i
        tk.op("T", emit_b2bc, reads=["cT", "b2_s"], writes=ybank(1024, 2048))
        tk.op("S", lambda: nc.scalar.copy(b2bc[:, :], Y[0:1, 1024:2048]),
              reads=ybank(1024, 2048), writes=["b2bc"])

        def emit_b2T():
            for h in range(DC):
                i = nc.tensor.transpose(Y[:, h:h + 1], b2bc[0:1, h * 128:(h + 1) * 128],
                                        idf[0:1, 0:1])
            return i
        tk.op("T", emit_b2T, reads=["b2bc", "idf"], writes=ybank(0, 8))
        tk.op("V", lambda: nc.vector.tensor_copy(b2bcT[:, :], Y[:, 0:DC]),
              reads=ybank(0, 8), writes=["b2bcT"])

    def xnT_rhs(h, c0, n):
        if c0 < TQ:
            return xnT_o[h][:, c0:c0 + n]
        return xnT_x[h][:, c0 - TQ:c0 - TQ + n]

    def xnT_name(h, c0):
        return (f"xnT_o{h}") if c0 < TQ else (f"xnT_x{h}")

    tk.barrier()
    with ExitStack() as p2:
        S = p2.enter_context(nc.psum_tensor("p2S", [128, 2048], F32))
        Y = p2.enter_context(nc.psum_tensor("p2Y", [128, 1536], F32))
        sbank = mkbank("p2S"); ybank = mkbank("p2Y")
        ck = sbuf(p2, "ck", [128, T]); sk = sbuf(p2, "sk", [128, T])
        kTp = sbuf(p2, "kT", [128, T], BF16)
        qTp = sbuf(p2, "qT", [128, TQ], BF16)
        wA0 = sbuf(p2, "wA0", [128, D], BF16); wA1 = sbuf(p2, "wA1", [128, D], BF16)
        wB0 = sbuf(p2, "wB0", [128, D], BF16); wB1 = sbuf(p2, "wB1", [128, D], BF16)
        wV0 = sbuf(p2, "wV0", [128, D], BF16)
        tmp1 = sbuf(p2, "tmp1", [128, 512]); tmp2 = sbuf(p2, "tmp2", [128, 512])
        ex0 = sbuf(p2, "ex0", [128, TQ], BF16); ex1 = sbuf(p2, "ex1", [128, TQ], BF16)
        rcp = sbuf(p2, "rcp", [128, 1])
        bv_s = sbuf(p2, "bv_s", [1, D])
        vtiles = [sbuf(p2, f"vp{i}", [128, 130], BF16) for i in range(TT_N)]

        tk.op("d", lambda: nc.sync.dma_start(ck[:, :], cosk[:, :]), writes=["ck"])
        tk.op("d", lambda: nc.sync.dma_start(sk[:, :], sink[:, :]), writes=["sk"])
        tk.op("d", lambda: nc.sync.dma_start(bv_s[:, :], bv_d[:, :]), writes=["bv_s"])
        for i in range(TT_N):
            tk.op("G", lambda i=i: nc.gpsimd.memset(vtiles[i][:, 64:65], 1.0), writes=[f"vp{i}"])
            tk.op("G", lambda i=i: nc.gpsimd.memset(vtiles[i][:, 129:130], 1.0), writes=[f"vp{i}"])

        def wload(dst, nm, w_, pair):
            src = w_[:, :].rearrange("(h p) (b c) -> p b h c", p=128, c=128)
            tk.op("g", lambda: nc.gpsimd.dma_start(dst[:, :], src[:, pair, :, :]), writes=[nm])

        for pair in range(NP):
            for nm, dst, w_ in [("wA0", wA0, wk), ("wA1", wA1, wkr_d), ("wB0", wB0, wq),
                                ("wB1", wB1, wqr_d), ("wV0", wV0, wv)]:
                wload(dst, nm, w_, pair)
            cbq, cbqr, cbk, cbkr = pair, DC + pair, 2 * DC + pair, 3 * DC + pair

            for c in range(T // 512):
                pa, pb_ = sbank(0, 512), sbank(512, 1024)

                def emit_k(c=c):
                    for h in range(DC):
                        i = nc.tensor.matmul(S[:, 0:512], wA0[:, h * 128:(h + 1) * 128],
                                             xnT_rhs(h, c * 512, 512),
                                             start=(h == 0), stop=(h == DC - 1))
                    return i
                rds = ["wA0"] + [xnT_name(h, c * 512) for h in range(DC)]
                tk.op("T", emit_k, reads=rds, writes=pa)

                def emit_kr(c=c):
                    for h in range(DC):
                        i = nc.tensor.matmul(S[:, 512:1024], wA1[:, h * 128:(h + 1) * 128],
                                             xnT_rhs(h, c * 512, 512),
                                             start=(h == 0), stop=(h == DC - 1))
                    return i
                tk.op("T", emit_kr, reads=["wA1"] + [xnT_name(h, c * 512) for h in range(DC)],
                      writes=pb_)
                tk.op("V", lambda c=c, cbk=cbk: nc.vector.scalar_tensor_tensor(
                    tmp1[:, :], S[:, 0:512], bqkT[:, cbk:cbk + 1],
                    ck[:, c * 512:(c + 1) * 512], ALU.add, ALU.mult),
                    reads=pa + ["bqkT", "ck"], writes=["tmp1"])
                tk.op("V", lambda c=c, cbkr=cbkr: nc.vector.scalar_tensor_tensor(
                    tmp2[:, :], S[:, 512:1024], bqkT[:, cbkr:cbkr + 1],
                    sk[:, c * 512:(c + 1) * 512], ALU.add, ALU.mult),
                    reads=pb_ + ["bqkT", "sk"], writes=["tmp2"])
                tk.op("V", lambda c=c: nc.vector.tensor_tensor(
                    kTp[:, c * 512:(c + 1) * 512], tmp1[:, :], tmp2[:, :], ALU.add),
                    reads=["tmp1", "tmp2"], writes=["kT"])
            for c in range(TQ // 512):
                pa, pb_ = sbank(1024, 1536), sbank(1536, 2048)

                def emit_q(c=c):
                    for h in range(DC):
                        i = nc.tensor.matmul(S[:, 1024:1536], wB0[:, h * 128:(h + 1) * 128],
                                             xnT_o[h][:, c * 512:(c + 1) * 512],
                                             start=(h == 0), stop=(h == DC - 1))
                    return i
                tk.op("T", emit_q, reads=["wB0"] + [f"xnT_o{h}" for h in range(DC)], writes=pa)

                def emit_qr(c=c):
                    for h in range(DC):
                        i = nc.tensor.matmul(S[:, 1536:2048], wB1[:, h * 128:(h + 1) * 128],
                                             xnT_o[h][:, c * 512:(c + 1) * 512],
                                             start=(h == 0), stop=(h == DC - 1))
                    return i
                tk.op("T", emit_qr, reads=["wB1"] + [f"xnT_o{h}" for h in range(DC)], writes=pb_)
                tk.op("V", lambda c=c, cbq=cbq: nc.vector.scalar_tensor_tensor(
                    tmp1[:, :], S[:, 1024:1536], bqkT[:, cbq:cbq + 1],
                    ck[:, c * 512:(c + 1) * 512], ALU.add, ALU.mult),
                    reads=pa + ["bqkT", "ck"], writes=["tmp1"])
                tk.op("V", lambda c=c, cbqr=cbqr: nc.vector.scalar_tensor_tensor(
                    tmp2[:, :], S[:, 1536:2048], bqkT[:, cbqr:cbqr + 1],
                    sk[:, c * 512:(c + 1) * 512], ALU.add, ALU.mult),
                    reads=pb_ + ["bqkT", "sk"], writes=["tmp2"])
                tk.op("V", lambda c=c: nc.vector.tensor_tensor(
                    qTp[:, c * 512:(c + 1) * 512], tmp1[:, :], tmp2[:, :], ALU.add),
                    reads=["tmp1", "tmp2"], writes=["qT"])
            for tt in range(TT_N):
                coff = (tt % 4) * 512
                bnk = sbank(coff, coff + 512)

                def emit_v(tt=tt, coff=coff):
                    for h in range(DC):
                        i = nc.tensor.matmul(S[:, coff:coff + 128], xnT_rhs(h, tt * 128, 128),
                                             wV0[:, h * 128:(h + 1) * 128],
                                             start=(h == 0), stop=False)
                    i = nc.tensor.matmul(S[:, coff:coff + 128], ones1[:, :],
                                         bv_s[0:1, pair * 128:(pair + 1) * 128],
                                         start=False, stop=True)
                    return i
                tk.op("T", emit_v,
                      reads=["wV0", "bv_s", "ones1"] + [xnT_name(h, tt * 128) for h in range(DC)],
                      writes=bnk)

                def emit_ve(tt=tt, coff=coff):
                    nc.scalar.copy(vtiles[tt][:, 0:64], S[:, coff:coff + 64])
                    return nc.scalar.copy(vtiles[tt][:, 65:129], S[:, coff + 64:coff + 128])
                tk.op("S", emit_ve, reads=bnk, writes=[f"vp{tt}"])
            for hh in range(2):
                r0 = hh * 64
                ybanks_all = ybank(0, 1536)
                for j in range(TT_N):
                    exb = [ex0, ex1][j % 2]; exn = ["ex0", "ex1"][j % 2]

                    def emit_sc(j=j, r0=r0):
                        for c in range(4):
                            i = nc.tensor.matmul(S[:, c * 512:(c + 1) * 512],
                                                 kTp[r0:r0 + 64, j * 128:(j + 1) * 128],
                                                 qTp[r0:r0 + 64, c * 512:(c + 1) * 512],
                                                 start=True, stop=True)
                        return i
                    tk.op("T", emit_sc, reads=["kT", "qT"], writes=sbank(0, 2048))
                    tk.op("S", lambda exb=exb: nc.scalar.activation(
                        exb[:, :], S[:, :], AF.Exp, scale=0.125),
                        reads=sbank(0, 2048), writes=[exn])

                    def emit_pv(j=j, hh=hh, exb=exb):
                        for qt in range(TQ_N):
                            col = (qt // 7) * 512 + (qt % 7) * 65
                            first = (j == 0) and (qt % 7 == 0)
                            last = (j == TT_N - 1) and (qt % 7 == 6 or qt == TQ_N - 1)
                            i = nc.tensor.matmul(Y[:, col:col + 65],
                                                 exb[:, qt * 128:(qt + 1) * 128],
                                                 vtiles[j][:, hh * 65:hh * 65 + 65],
                                                 start=first, stop=last)
                        return i
                    tk.op("T", emit_pv, reads=[exn, f"vp{j}"], writes=ybanks_all)
                for qt in range(TQ_N):
                    col = (qt // 7) * 512 + (qt % 7) * 65
                    tk.op("V", lambda col=col: nc.vector.reciprocal(
                        rcp[:, :], Y[:, col + 64:col + 65]),
                        reads=ybanks_all, writes=["rcp"])
                    hcol = (pair * 2 + hh) * 64
                    tk.op("S", lambda qt=qt, col=col, hcol=hcol: nc.scalar.activation(
                        y_acc[qt][:, hcol:hcol + 64], Y[:, col:col + 64], AF.Identity,
                        scale=rcp[:, :]),
                        reads=ybanks_all + ["rcp"], writes=[f"y{qt}"])
    es1.close()

    tk.barrier()
    with ExitStack() as p3:
        PT = p3.enter_context(nc.psum_tensor("p3T", [128, 2048], BF16))
        Y = p3.enter_context(nc.psum_tensor("p3Y", [128, 2048], F32))
        tbank = mkbank("p3T", 2); ybank = mkbank("p3Y")
        woc = sbuf(p3, "woc", [128, D * DC], BF16)
        xo0 = sbuf(p3, "xo0", [128, D]); xo1 = sbuf(p3, "xo1", [128, D])
        ao0 = sbuf(p3, "ao0", [128, D]); ao1 = sbuf(p3, "ao1", [128, D])
        bo_s = sbuf(p3, "bo_s", [1, D])
        yT = [sbuf(p3, f"yT{i}", [128, TQ], BF16) for i in range(DC)]
        tk.op("d", lambda: nc.sync.dma_start(bo_s[:, :], bo_d[:, :]), writes=["bo_s"])
        for h in range(DC):
            tk.op("g", lambda h=h: nc.gpsimd.dma_start(
                woc[:, h * D:(h + 1) * D], wo[h * 128:(h + 1) * 128, :]), writes=["woc"])
        for qt in range(TQ_N):
            half = qt % 2
            cb = tbank(half * 1024, half * 1024 + 1024)

            def emit_ytr(qt=qt, half=half):
                for h in range(DC):
                    i = nc.tensor.transpose(
                        PT[:, half * 1024 + h * 128:half * 1024 + (h + 1) * 128],
                        y_acc[qt][:, h * 128:(h + 1) * 128], idb[:, :])
                return i
            tk.op("T", emit_ytr, reads=[f"y{qt}", "idb"], writes=cb)

            def emit_yev(qt=qt, half=half):
                for h in range(DC):
                    i = nc.vector.tensor_copy(
                        yT[h][:, qt * 128:(qt + 1) * 128],
                        PT[:, half * 1024 + h * 128:half * 1024 + (h + 1) * 128])
                return i
            tk.op("V", emit_yev, reads=cb, writes=[f"yT{h}" for h in range(DC)])
        xos = [xo0, xo1]; aos = [ao0, ao1]
        for qt in range(TQ_N):
            xo = xos[qt % 2]; ao = aos[qt % 2]
            xnm = f"xo{qt % 2}"; anm = f"ao{qt % 2}"
            tk.op("d", lambda qt=qt, xo=xo: nc.sync.dma_start(
                xo[:, :], xb[qt * 128:(qt + 1) * 128, :]), writes=[xnm])
            half = qt % 2
            cb = ybank(half * 1024, half * 1024 + 1024)

            def emit_o(qt=qt, half=half):
                for c in range(2):
                    for h in range(DC):
                        i = nc.tensor.matmul(
                            Y[:, half * 1024 + c * 512:half * 1024 + (c + 1) * 512],
                            yT[h][:, qt * 128:(qt + 1) * 128],
                            woc[:, h * D + c * 512:h * D + (c + 1) * 512],
                            start=(h == 0), stop=False)
                    i = nc.tensor.matmul(
                        Y[:, half * 1024 + c * 512:half * 1024 + (c + 1) * 512],
                        ones1[:, :], bo_s[0:1, c * 512:(c + 1) * 512],
                        start=False, stop=True)
                return i
            tk.op("T", emit_o,
                  reads=[f"yT{h}" for h in range(DC)] + ["woc", "ones1", "bo_s"], writes=cb)
            tk.op("V", lambda ao=ao, xo=xo, half=half: nc.vector.tensor_tensor(
                ao[:, :], Y[:, half * 1024:half * 1024 + 1024], xo[:, :], ALU.add),
                reads=cb + [xnm], writes=[anm])
            tk.op("d", lambda qt=qt, ao=ao: nc.sync.dma_start(
                out_c[qt * 128:(qt + 1) * 128, :], ao[:, :]),
                reads=[anm], writes=[f"oc{qt}"], sem_key=f"st{qt % 2}")
    es2.close()

    tk.barrier()
    with ExitStack() as p4:
        S = p4.enter_context(nc.psum_tensor("p4S", [128, 2048], F32))
        Y = p4.enter_context(nc.psum_tensor("p4Y", [128, 2048], F32))
        sbank = mkbank("p4S"); ybank = mkbank("p4Y")
        w1c0 = sbuf(p4, "w1c0", [128, D], BF16); w1c1 = sbuf(p4, "w1c1", [128, D], BF16)
        w2c0 = sbuf(p4, "w2c0", [128, FF], BF16); w2c1 = sbuf(p4, "w2c1", [128, FF], BF16)
        b1raw = sbuf(p4, "b1raw", [FC, 128])
        ffn = [sbuf(p4, f"ffn{i}", [128, TQ]) for i in range(DC)]
        hmid = [sbuf(p4, f"hm{i}", [128, WIN], BF16) for i in range(FC)]
        for h in range(DC):
            tk.op("S", lambda h=h: nc.scalar.activation(
                ffn[h][:, :], xnT_o[0][:, :], AF.Identity,
                bias=b2bcT[:, h:h + 1], scale=0.0),
                reads=["b2bcT", "xnT_o0"], writes=[f"ffn{h}"])
        w1cs = [w1c0, w1c1]; w1ns = ["w1c0", "w1c1"]
        w2cs = [w2c0, w2c1]; w2ns = ["w2c0", "w2c1"]
        for e in range(NE):
            tk.op("d", lambda e=e: nc.sync.dma_start(b1raw[:, :], b1[e, :, :]), writes=["b1raw"])
            tk.op("T", lambda: nc.tensor.transpose(Y[:, 1024:1024 + FC], b1raw[:, :],
                                                   idf[0:FC, 0:FC]),
                  reads=["b1raw", "idf"], writes=ybank(1024, 1024 + FC))
            tk.op("V", lambda: nc.vector.tensor_copy(b1T[:, :], Y[:, 1024:1024 + FC]),
                  reads=ybank(1024, 1024 + FC), writes=["b1T"])
            for w in range(NW):
                for f in range(FC):
                    wb = w1cs[f % 2]; wn = w1ns[f % 2]
                    tk.op("g", lambda wb=wb, e=e, f=f: nc.gpsimd.dma_start(
                        wb[:, :], w1r[e, f].rearrange("(h p) c -> p h c", p=128)),
                        writes=[wn])
                    pcols = (f % 2) * 1024
                    pb_ = sbank(pcols, pcols + 1024)

                    def emit_m1(w=w, wb=wb, pcols=pcols):
                        for h in range(DC):
                            for c in range(2):
                                i = nc.tensor.matmul(
                                    S[:, pcols + c * 512:pcols + (c + 1) * 512],
                                    wb[:, h * 128:(h + 1) * 128],
                                    xnT_o[h][:, w * WIN + c * 512:w * WIN + (c + 1) * 512],
                                    start=(h == 0), stop=(h == DC - 1))
                        return i
                    tk.op("T", emit_m1, reads=[wn] + [f"xnT_o{h}" for h in range(DC)],
                          writes=pb_)
                    tk.op("S", lambda f=f, pcols=pcols: nc.scalar.activation(
                        hmid[f][:, :], S[:, pcols:pcols + 1024], AF.Gelu,
                        bias=b1T[:, f:f + 1]),
                        reads=pb_ + ["b1T"], writes=[f"hm{f}"])
                for ho in range(DC):
                    wb = w2cs[ho % 2]; wn = w2ns[ho % 2]
                    tk.op("g", lambda wb=wb, e=e, ho=ho: nc.gpsimd.dma_start(
                        wb[:, :], w2r[e, ho].rearrange("(f p) c -> p f c", p=128)),
                        writes=[wn])
                    pcols = (ho % 2) * 1024
                    pb_ = ybank(pcols, pcols + 1024)

                    def emit_m2(wb=wb, pcols=pcols):
                        for f in range(FC):
                            for c in range(2):
                                i = nc.tensor.matmul(
                                    Y[:, pcols + c * 512:pcols + (c + 1) * 512],
                                    wb[:, f * 128:(f + 1) * 128],
                                    hmid[f][:, c * 512:(c + 1) * 512],
                                    start=(f == 0), stop=(f == FC - 1))
                        return i
                    tk.op("T", emit_m2, reads=[wn] + [f"hm{f}" for f in range(FC)],
                          writes=pb_)
                    tk.op("V", lambda ho=ho, w=w, e=e, pcols=pcols: nc.vector.scalar_tensor_tensor(
                        ffn[ho][:, w * WIN:(w + 1) * WIN], Y[:, pcols:pcols + 1024],
                        cbc[:, e:e + 1], ffn[ho][:, w * WIN:(w + 1) * WIN],
                        ALU.mult, ALU.add),
                        reads=pb_ + ["cbc", f"ffn{ho}"], writes=[f"ffn{ho}"])
        with ExitStack() as p5:
            fo0 = sbuf(p5, "fo0", [128, D]); fo1 = sbuf(p5, "fo1", [128, D])
            go0 = sbuf(p5, "go0", [128, D]); go1 = sbuf(p5, "go1", [128, D])
            fos = [fo0, fo1]; gos = [go0, go1]
            for qt in range(TQ_N):
                fo = fos[qt % 2]; go = gos[qt % 2]
                fnm = f"fo{qt % 2}"; gnm = f"go{qt % 2}"
                tk.op("d", lambda qt=qt, fo=fo: nc.sync.dma_start(
                    fo[:, :], out_c[qt * 128:(qt + 1) * 128, :]),
                    reads=[f"oc{qt}"], writes=[fnm])
                half = qt % 2
                cb = sbank(half * 1024, half * 1024 + 1024)

                def emit_ftr(qt=qt, half=half):
                    for h in range(DC):
                        i = nc.tensor.transpose(
                            S[:, half * 1024 + h * 128:half * 1024 + (h + 1) * 128],
                            ffn[h][:, qt * 128:(qt + 1) * 128], idf[:, :])
                    return i
                tk.op("T", emit_ftr, reads=[f"ffn{h}" for h in range(DC)] + ["idf"],
                      writes=cb)
                tk.op("V", lambda go=go, fo=fo, half=half: nc.vector.tensor_tensor(
                    go[:, :], S[:, half * 1024:half * 1024 + 1024], fo[:, :], ALU.add),
                    reads=cb + [fnm], writes=[gnm])
                tk.op("d", lambda qt=qt, go=go: nc.sync.dma_start(
                    out_c[qt * 128:(qt + 1) * 128, :], go[:, :]),
                    reads=[gnm], writes=[f"oc{qt}"], sem_key=f"sf{qt % 2}")
    tk.final_waits("d")
    st.close()
    return nc


def _rope_tables():
    half = HD // 2
    pos = np.arange(T, dtype=np.float64)
    idx = np.arange(half, dtype=np.float64)
    inv = 1.0 / (10000.0 ** (idx / half))
    ang = pos[None, :] * inv[:, None]
    cos = np.cos(ang); sin = np.sin(ang)
    c64 = np.concatenate([cos, cos], 0)
    s64 = np.concatenate([-sin, sin], 0)
    c128 = np.concatenate([c64, c64], 0).astype(np.float32)
    s128 = np.concatenate([s64, s64], 0).astype(np.float32)
    return c128, s128


def kernel(L0_attn_x, L0_attn_wq_w, L0_attn_wq_b, L0_attn_wk_w, L0_attn_wk_b,
           L0_attn_wv_w, L0_attn_wv_b, L0_attn_wo_w, L0_attn_wo_b,
           L0_ffn_gate_w, L0_ffn_w1, L0_ffn_b1, L0_ffn_w2, L0_ffn_b2):
    f32c = lambda a: np.ascontiguousarray(np.asarray(a, dtype=np.float32))
    x = f32c(L0_attn_x)
    wq, wk, wv, wo = f32c(L0_attn_wq_w), f32c(L0_attn_wk_w), f32c(L0_attn_wv_w), f32c(L0_attn_wo_w)
    bq, bk, bv, bo = f32c(L0_attn_wq_b), f32c(L0_attn_wk_b), f32c(L0_attn_wv_b), f32c(L0_attn_wo_b)
    gate = f32c(L0_ffn_gate_w)
    w1, b1, w2, b2 = f32c(L0_ffn_w1), f32c(L0_ffn_b1), f32c(L0_ffn_w2), f32c(L0_ffn_b2)

    do = np.arange(D)
    perm = (do - (do % HD)) + ((do % HD) + 32) % HD
    wqr = np.ascontiguousarray(wq[:, perm]); wkr = np.ascontiguousarray(wk[:, perm])
    bqr = bq[perm]; bkr = bk[perm]
    bqkT = np.ascontiguousarray(
        np.concatenate([bq.reshape(DC, 128), bqr.reshape(DC, 128),
                        bk.reshape(DC, 128), bkr.reshape(DC, 128)], 0))

    w1r = np.ascontiguousarray(w1.reshape(NE, D, FC, 128).transpose(0, 2, 1, 3))
    w2r = np.ascontiguousarray(w2.reshape(NE, FF, DC, 128).transpose(0, 2, 1, 3))
    b1r = np.ascontiguousarray(b1.reshape(NE, FC, 128))

    c128, s128 = _rope_tables()
    shared = dict(
        wq=wq, wqr=wqr, wk=wk, wkr=wkr, wv=wv, wo=wo,
        bqkTd=bqkT, bv=bv.reshape(1, D), bo=bo.reshape(1, D),
        gate=gate, w1r=w1r, w2r=w2r, b1=b1r, b2=b2,
        ident=np.eye(128, dtype=np.float32),
        identb=np.eye(128).astype(ml_dtypes.bfloat16))

    in_maps = []
    for c in range(8):
        b, h = c // 2, c % 2
        xb = np.ascontiguousarray(
            np.concatenate([x[b, h * TQ:(h + 1) * TQ], x[b, (1 - h) * TQ:(2 - h) * TQ]], 0))
        tord = np.concatenate([np.arange(h * TQ, (h + 1) * TQ),
                               np.arange((1 - h) * TQ, (2 - h) * TQ)])
        m = dict(shared)
        m["xb"] = xb
        m["cosk"] = np.ascontiguousarray(c128[:, tord])
        m["sink"] = np.ascontiguousarray(s128[:, tord])
        in_maps.append(m)

    nc = build()
    res = run_bass_kernel_spmd(nc, in_maps, core_ids=list(range(8)))
    out = np.empty((4, T, D), dtype=np.float32)
    for c in range(8):
        b, h = c // 2, c % 2
        out[b, h * TQ:(h + 1) * TQ] = res.results[c]["out_c"]
    return out


# version 04
# speedup vs baseline: 1.0002x; 1.0002x over previous
import numpy as np
import ml_dtypes
from contextlib import ExitStack
import concourse.bass as bass
import concourse.mybir as mybir
from concourse.bass_utils import run_bass_kernel_spmd

F32 = mybir.dt.float32
BF16 = mybir.dt.bfloat16
AF = mybir.ActivationFunctionType
ALU = mybir.AluOpType
AX = mybir.AxisListType

D = 1024
T = 4096
TQ = 2048
NH = 16
HD = 64
NP = 8
NE = 8
FF = 4096
DC = D // 128
FC = FF // 128
TT_N = T // 128
TQ_N = TQ // 128
WIN = 1024
NW = TQ // WIN


class Tk:

    def __init__(self, nc):
        self.nc = nc
        self.e = {}
        self.res = {}

    def add_engine(self, name, obj, sem, step):
        self.e[name] = {"o": obj, "sem": sem, "step": step, "cnt": 0, "waited": {}}

    def _need(self, deps, tok):
        if tok is None:
            return
        en, v = tok
        if deps.get(en, 0) < v:
            deps[en] = v

    def _bufsem(self, key, issuer):
        nm = f"D:{key}"
        if nm not in self.e:
            sem = self.nc.semaphore(f"ds_{key}").__enter__()
            self.e[nm] = {"o": None, "sem": sem, "step": 16, "cnt": 0, "waited": {}}
        return nm

    def op(self, ename, emit, reads=(), writes=(), sem_key=None):
        e = self.e[ename]
        is_dma = ename in ("d", "g")
        deps = {}
        for r in reads:
            rr = self.res.get(r)
            if rr:
                self._need(deps, rr["w"])
        for w in writes:
            rr = self.res.get(w)
            if rr:
                self._need(deps, rr["w"])
                for t in rr["r"]:
                    self._need(deps, t)
        for pn, v in sorted(deps.items()):
            if pn == ename and ename == "T":
                continue
            if e["waited"].get(pn, 0) >= v:
                continue
            pe = self.e[pn]
            assert v <= pe["cnt"], (ename, pn, v, pe["cnt"])
            e["o"].wait_ge(pe["sem"], v)
            e["waited"][pn] = v
        inst = emit()
        if is_dma:
            snm = self._bufsem(sem_key if sem_key is not None else writes[0], ename)
            se = self.e[snm]
            inst.then_inc(se["sem"], 16)
            se["cnt"] += 16
            tok = (snm, se["cnt"])
            e["waited"][snm] = 0
        else:
            inst.then_inc(e["sem"], e["step"])
            e["cnt"] += e["step"]
            tok = (ename, e["cnt"])
        for w in writes:
            self.res[w] = {"w": tok, "r": []}
        for r in reads:
            rr = self.res.setdefault(r, {"w": None, "r": []})
            rr["r"].append(tok)
        return inst

    def barrier(self):
        for en, e in self.e.items():
            if e["o"] is None:
                continue
            for pn, pe in self.e.items():
                if pn == en or pe["cnt"] == 0:
                    continue
                if e["waited"].get(pn, 0) < pe["cnt"]:
                    e["o"].wait_ge(pe["sem"], pe["cnt"])
                    e["waited"][pn] = pe["cnt"]

    def final_waits(self, ename):
        e = self.e[ename]
        for pn, pe in self.e.items():
            if pn == ename or pe["cnt"] == 0 or pe["o"] is e["o"]:
                continue
            if e["waited"].get(pn, 0) < pe["cnt"]:
                e["o"].wait_ge(pe["sem"], pe["cnt"])
                e["waited"][pn] = pe["cnt"]


def build():
    nc = bass.Bass("TRN2", target_bir_lowering=False)
    dt = lambda n, s, d=F32, k="ExternalInput": nc.dram_tensor(n, s, d, kind=k)

    xb = dt("xb", [T, D])
    wq = dt("wq", [D, D]); wqr_d = dt("wqr", [D, D])
    wk = dt("wk", [D, D]); wkr_d = dt("wkr", [D, D])
    wv = dt("wv", [D, D]); wo = dt("wo", [D, D])
    bqkT_d = dt("bqkTd", [4 * DC, 128])
    bv_d = dt("bv", [1, D]); bo_d = dt("bo", [1, D])
    gate = dt("gate", [D, NE])
    w1r = dt("w1r", [NE, FC, D, 128])
    w2r = dt("w2r", [NE, DC, FF, 128])
    b1 = dt("b1", [NE, FC, 128])
    b2 = dt("b2", [NE, D])
    cosk = dt("cosk", [128, T]); sink = dt("sink", [128, T])
    coskl = dt("coskl", [128, T]); sinkl = dt("sinkl", [128, T])
    ident = dt("ident", [128, 128])
    identb = dt("identb", [128, 128], BF16)
    out_c = dt("out_c", [TQ, D], F32, k="ExternalOutput")

    tk = Tk(nc)
    st = ExitStack()
    sem = lambda n: st.enter_context(nc.semaphore(n))
    ct, cv, cs, cg = sem("ct"), sem("cv"), sem("cs"), sem("cg")
    sd, sg = sem("sd"), sem("sg")
    tk.add_engine("T", nc.tensor, ct, 1)
    tk.add_engine("V", nc.vector, cv, 1)
    tk.add_engine("S", nc.scalar, cs, 1)
    tk.add_engine("G", nc.gpsimd, cg, 1)
    tk.add_engine("d", nc.sync, sd, 16)
    tk.add_engine("g", nc.gpsimd, sg, 16)

    sbuf = lambda es, n, s, d=F32: es.enter_context(nc.sbuf_tensor(n, s, d))

    xnT_o = [sbuf(st, f"xnT_o{i}", [128, TQ], BF16) for i in range(DC)]
    idf = sbuf(st, "idf", [128, 128])
    idb = sbuf(st, "idb", [128, 128], BF16)
    ones1 = sbuf(st, "ones1", [1, 128])
    onesc = sbuf(st, "onesc", [128, 1])
    gate_s = sbuf(st, "gate_s", [128, DC * NE])
    comb = sbuf(st, "comb", [1, NE])
    cbc = sbuf(st, "cbc", [128, NE])
    b2bcT = sbuf(st, "b2bcT", [128, DC])
    bqkT = sbuf(st, "bqkT", [128, 4 * DC])
    logit = sbuf(st, "logit", [1, NE])
    rsm = sbuf(st, "rsm", [1, 8])
    rmask = sbuf(st, "rmask", [1, NE])
    rl2 = sbuf(st, "rl2", [1, NE])
    rexp = sbuf(st, "rexp", [1, NE])
    seqT = sbuf(st, "seqT", [128, DC])
    b1T = sbuf(st, "b1T", [128, FC])

    def mkbank(tag, elsize=4):
        per = 2048 // elsize
        return lambda lo, hi: [f"{tag}{i}" for i in range(lo // per, (hi + per - 1) // per)]

    tk.op("d", lambda: nc.sync.dma_start(idf[:, :], ident[:, :]), writes=["idf"])
    tk.op("d", lambda: nc.sync.dma_start(idb[:, :], identb[:, :]), writes=["idb"])
    for h in range(DC):
        tk.op("g", lambda h=h: nc.gpsimd.dma_start(
            gate_s[:, h * NE:(h + 1) * NE], gate[h * 128:(h + 1) * 128, :],
            allow_slow_non_contiguous=True), writes=["gate_s"])
    tk.op("G", lambda: nc.gpsimd.memset(ones1[:, :], 1.0), writes=["ones1"])
    tk.op("G", lambda: nc.gpsimd.memset(onesc[:, :], 1.0), writes=["onesc"])

    es2 = ExitStack()
    y_acc = [sbuf(es2, f"y{i}", [128, D], BF16) for i in range(TQ_N)]
    es1 = ExitStack()
    xnT_x = [sbuf(es1, f"xnT_x{i}", [128, TQ], BF16) for i in range(DC)]

    with ExitStack() as p1:
        S = p1.enter_context(nc.psum_tensor("p1S", [128, 1024], F32))
        Y = p1.enter_context(nc.psum_tensor("p1Y", [128, 2048], F32))
        sbank = mkbank("p1S"); ybank = mkbank("p1Y")
        xt0 = sbuf(p1, "xt0", [128, D]); xt1 = sbuf(p1, "xt1", [128, D])
        xnt = sbuf(p1, "xnt", [128, D]); xsum = sbuf(p1, "xsum", [128, D])
        stat = sbuf(p1, "stat", [128, 12]); mv = sbuf(p1, "mv", [128, 2])
        rstd = sbuf(p1, "rstd", [128, 1]); nmr = sbuf(p1, "nmr", [128, 1])
        seqrow = sbuf(p1, "seqrow", [1, D])
        cT = sbuf(p1, "cT", [NE, 1]); b2_s = sbuf(p1, "b2_s", [NE, D])
        b2bc = sbuf(p1, "b2bc", [1, D]); lgT = sbuf(p1, "lgT", [NE, 1])
        braw = sbuf(p1, "braw", [4 * DC, 128])

        tk.op("d", lambda: nc.sync.dma_start(braw[:, :], bqkT_d[:, :]), writes=["braw"])
        tk.op("T", lambda: nc.tensor.transpose(Y[:, 0:4 * DC], braw[:, :], idf[0:4 * DC, 0:4 * DC]),
              reads=["braw", "idf"], writes=ybank(0, 32))
        tk.op("V", lambda: nc.vector.tensor_copy(bqkT[:, :], Y[:, 0:4 * DC]),
              reads=ybank(0, 32), writes=["bqkT"])

        xts = [xt0, xt1]
        for tt in range(TT_N):
            xt = xts[tt % 2]; xnm = f"xt{tt % 2}"
            tk.op("d", lambda tt=tt, xt=xt: nc.sync.dma_start(
                xt[:, :], xb[tt * 128:(tt + 1) * 128, :]), writes=[xnm])
            tk.op("V", lambda xt=xt: nc.vector.bn_stats(stat[:, 0:6], xt[:, 0:512]),
                  reads=[xnm], writes=["stat"])
            tk.op("V", lambda xt=xt: nc.vector.bn_stats(stat[:, 6:12], xt[:, 512:1024]),
                  reads=[xnm], writes=["stat2"])
            tk.op("V", lambda: nc.vector.bn_aggr(mv[:, :], stat[:, :]),
                  reads=["stat", "stat2"], writes=["mv"])
            tk.op("V", lambda: nc.vector.tensor_scalar_add(rstd[:, :], mv[:, 1:2], 1e-5),
                  reads=["mv"], writes=["rstd"])
            tk.op("S", lambda: nc.scalar.activation(rstd[:, :], rstd[:, :], AF.Sqrt),
                  reads=["rstd"], writes=["rstd"])
            tk.op("V", lambda: nc.vector.reciprocal(rstd[:, :], rstd[:, :]),
                  reads=["rstd"], writes=["rstd"])
            tk.op("V", lambda: nc.vector.tensor_tensor(nmr[:, :], mv[:, 0:1], rstd[:, :], ALU.mult),
                  reads=["mv", "rstd"], writes=["nmr"])
            tk.op("V", lambda: nc.vector.tensor_scalar_mul(nmr[:, :], nmr[:, :], -1.0),
                  reads=["nmr"], writes=["nmr"])
            tk.op("S", lambda xt=xt: nc.scalar.activation(
                xnt[:, :], xt[:, :], AF.Identity, bias=nmr[:, :], scale=rstd[:, :]),
                reads=[xnm, "nmr", "rstd"], writes=["xnt"])
            if tt == 0:
                tk.op("V", lambda: nc.vector.tensor_copy(xsum[:, :], xnt[:, :]),
                      reads=["xnt"], writes=["xsum"])
            else:
                tk.op("V", lambda: nc.vector.tensor_tensor(xsum[:, :], xsum[:, :], xnt[:, :], ALU.add),
                      reads=["xnt", "xsum"], writes=["xsum"])

            def emit_tr():
                for h in range(DC):
                    i = nc.tensor.transpose(S[:, h * 128:(h + 1) * 128],
                                            xnt[:, h * 128:(h + 1) * 128], idf[:, :])
                return i
            tk.op("T", emit_tr, reads=["xnt", "idf"], writes=sbank(0, 1024))
            own = tt < TQ_N
            dsts = xnT_o if own else xnT_x
            col = (tt if own else tt - TQ_N) * 128

            def emit_cp(dsts=dsts, col=col):
                for h in range(DC):
                    i = nc.vector.tensor_copy(dsts[h][:, col:col + 128],
                                              S[:, h * 128:(h + 1) * 128])
                return i
            tk.op("V", emit_cp, reads=sbank(0, 1024),
                  writes=[("xnT_o" if own else "xnT_x") + str(h) for h in range(DC)])

        def emit_seq():
            for c in range(2):
                i = nc.tensor.matmul(S[0:1, c * 512:(c + 1) * 512], onesc[:, :],
                                     xsum[:, c * 512:(c + 1) * 512], start=True, stop=True)
            return i
        tk.op("T", emit_seq, reads=["onesc", "xsum"], writes=sbank(0, 1024))
        tk.op("S", lambda: nc.scalar.copy(seqrow[:, :], S[0:1, 0:1024]),
              reads=sbank(0, 1024), writes=["seqrow"])

        def emit_seqT():
            for h in range(DC):
                i = nc.tensor.transpose(Y[:, h:h + 1], seqrow[0:1, h * 128:(h + 1) * 128],
                                        idf[0:1, 0:1])
            return i
        tk.op("T", emit_seqT, reads=["seqrow", "idf"], writes=ybank(0, 8))
        tk.op("V", lambda: nc.vector.tensor_copy(seqT[:, :], Y[:, 0:DC]),
              reads=ybank(0, 8), writes=["seqT"])

        def emit_log():
            for h in range(DC):
                i = nc.tensor.matmul(Y[0:NE, 512:513], gate_s[:, h * NE:(h + 1) * NE],
                                     seqT[:, h:h + 1], start=(h == 0), stop=(h == DC - 1))
            return i
        tk.op("T", emit_log, reads=["gate_s", "seqT"], writes=ybank(512, 513))
        tk.op("S", lambda: nc.scalar.activation(lgT[:, :], Y[0:NE, 512:513],
                                                AF.Identity, scale=1.0 / T),
              reads=ybank(512, 513), writes=["lgT"])
        tk.op("T", lambda: nc.tensor.transpose(Y[0:1, 0:NE], lgT[:, :], idf[0:NE, 0:NE]),
              reads=["lgT", "idf"], writes=ybank(0, 8))
        tk.op("V", lambda: nc.vector.tensor_copy(logit[:, :], Y[0:1, 0:NE]),
              reads=ybank(0, 8), writes=["logit"])
        tk.op("V", lambda: nc.vector.reduce_max(rsm[:, 0:1], logit[:, :], axis=AX.X),
              reads=["logit"], writes=["rm1"])
        tk.op("V", lambda: nc.vector.tensor_scalar(rmask[:, :], logit[:, :], rsm[:, 0:1],
                                                   None, op0=ALU.is_ge),
              reads=["logit", "rm1"], writes=["rmask"])
        tk.op("V", lambda: nc.vector.scalar_tensor_tensor(rl2[:, :], rmask[:, :], -1e9,
                                                          logit[:, :], ALU.mult, ALU.add),
              reads=["rmask", "logit"], writes=["rl2"])
        tk.op("V", lambda: nc.vector.reduce_max(rsm[:, 2:3], rl2[:, :], axis=AX.X),
              reads=["rl2"], writes=["rm2"])
        tk.op("V", lambda: nc.vector.tensor_scalar(rl2[:, :], rl2[:, :], rsm[:, 2:3],
                                                   None, op0=ALU.is_ge),
              reads=["rl2", "rm2"], writes=["rl2"])
        tk.op("V", lambda: nc.vector.tensor_tensor(rmask[:, :], rmask[:, :], rl2[:, :], ALU.add),
              reads=["rmask", "rl2"], writes=["rmask"])
        tk.op("V", lambda: nc.vector.tensor_scalar_mul(rsm[:, 1:2], rsm[:, 0:1], -1.0),
              reads=["rm1"], writes=["rm1n"])
        tk.op("S", lambda: nc.scalar.activation(rexp[:, :], logit[:, :], AF.Exp, bias=rsm[:, 1:2]),
              reads=["logit", "rm1n"], writes=["rexp"])
        tk.op("V", lambda: nc.vector.tensor_tensor(rexp[:, :], rexp[:, :], rmask[:, :], ALU.mult),
              reads=["rexp", "rmask"], writes=["rexp"])
        tk.op("V", lambda: nc.vector.reduce_sum(rsm[:, 3:4], rexp[:, :], axis=AX.X),
              reads=["rexp"], writes=["rden"])
        tk.op("V", lambda: nc.vector.reciprocal(rsm[:, 4:5], rsm[:, 3:4]),
              reads=["rden"], writes=["rrec"])
        tk.op("V", lambda: nc.vector.tensor_scalar(comb[:, :], rexp[:, :], rsm[:, 4:5],
                                                   None, op0=ALU.mult),
              reads=["rexp", "rrec"], writes=["comb"])
        tk.op("T", lambda: nc.tensor.matmul(Y[:, 0:NE], ones1[:, :], comb[:, :],
                                            start=True, stop=True),
              reads=["ones1", "comb"], writes=ybank(0, 8))
        tk.op("V", lambda: nc.vector.tensor_copy(cbc[:, :], Y[:, 0:NE]),
              reads=ybank(0, 8), writes=["cbc"])
        tk.op("T", lambda: nc.tensor.transpose(Y[0:NE, 512:513], comb[0:1, :], idf[0:1, 0:1]),
              reads=["comb", "idf"], writes=ybank(512, 513))
        tk.op("V", lambda: nc.vector.tensor_copy(cT[:, :], Y[0:NE, 512:513]),
              reads=ybank(512, 513), writes=["cT"])
        tk.op("d", lambda: nc.sync.dma_start(b2_s[:, :], b2[:, :]), writes=["b2_s"])

        def emit_b2bc():
            for c in range(2):
                i = nc.tensor.matmul(Y[0:1, 1024 + c * 512:1024 + (c + 1) * 512], cT[:, :],
                                     b2_s[:, c * 512:(c + 1) * 512], start=True, stop=True)
            return i
        tk.op("T", emit_b2bc, reads=["cT", "b2_s"], writes=ybank(1024, 2048))
        tk.op("S", lambda: nc.scalar.copy(b2bc[:, :], Y[0:1, 1024:2048]),
              reads=ybank(1024, 2048), writes=["b2bc"])

        def emit_b2T():
            for h in range(DC):
                i = nc.tensor.transpose(Y[:, h:h + 1], b2bc[0:1, h * 128:(h + 1) * 128],
                                        idf[0:1, 0:1])
            return i
        tk.op("T", emit_b2T, reads=["b2bc", "idf"], writes=ybank(0, 8))
        tk.op("V", lambda: nc.vector.tensor_copy(b2bcT[:, :], Y[:, 0:DC]),
              reads=ybank(0, 8), writes=["b2bcT"])

    def xnT_rhs(h, c0, n):
        if c0 < TQ:
            return xnT_o[h][:, c0:c0 + n]
        return xnT_x[h][:, c0 - TQ:c0 - TQ + n]

    def xnT_name(h, c0):
        return (f"xnT_o{h}") if c0 < TQ else (f"xnT_x{h}")

    tk.barrier()
    with ExitStack() as p2:
        S = p2.enter_context(nc.psum_tensor("p2S", [128, 2048], F32))
        Y = p2.enter_context(nc.psum_tensor("p2Y", [128, 1536], F32))
        sbank = mkbank("p2S"); ybank = mkbank("p2Y")
        ck = sbuf(p2, "ck", [128, T]); sk = sbuf(p2, "sk", [128, T])
        kTp = sbuf(p2, "kT", [128, T], BF16)
        qTp = sbuf(p2, "qT", [128, TQ], BF16)
        wA0 = sbuf(p2, "wA0", [128, D], BF16); wA1 = sbuf(p2, "wA1", [128, D], BF16)
        wB0 = sbuf(p2, "wB0", [128, D], BF16); wB1 = sbuf(p2, "wB1", [128, D], BF16)
        wV0 = sbuf(p2, "wV0", [128, D], BF16)
        tmp1 = sbuf(p2, "tmp1", [128, 512]); tmp2 = sbuf(p2, "tmp2", [128, 512])
        ex0 = sbuf(p2, "ex0", [128, TQ], BF16); ex1 = sbuf(p2, "ex1", [128, TQ], BF16)
        rcp = sbuf(p2, "rcp", [128, 1])
        bv_s = sbuf(p2, "bv_s", [1, D])
        vtiles = [sbuf(p2, f"vp{i}", [128, 130], BF16) for i in range(TT_N)]

        tk.op("d", lambda: nc.sync.dma_start(ck[:, :], cosk[:, :]), writes=["ck"])
        tk.op("d", lambda: nc.sync.dma_start(sk[:, :], sink[:, :]), writes=["sk"])
        tlo = sbuf(p2, "tlo", [128, 2048])
        for dst, dnm, lo_d in ((ck, "ck", coskl), (sk, "sk", sinkl)):
            for half in range(2):
                tk.op("d", lambda lo_d=lo_d, half=half: nc.sync.dma_start(
                    tlo[:, :], lo_d[:, half * 2048:(half + 1) * 2048]), writes=["tlo"])
                tk.op("V", lambda dst=dst, half=half: nc.vector.tensor_tensor(
                    dst[:, half * 2048:(half + 1) * 2048],
                    dst[:, half * 2048:(half + 1) * 2048], tlo[:, :], ALU.add),
                    reads=["tlo", dnm], writes=[dnm])
        tk.op("d", lambda: nc.sync.dma_start(bv_s[:, :], bv_d[:, :]), writes=["bv_s"])
        for i in range(TT_N):
            tk.op("G", lambda i=i: nc.gpsimd.memset(vtiles[i][:, 64:65], 1.0), writes=[f"vp{i}"])
            tk.op("G", lambda i=i: nc.gpsimd.memset(vtiles[i][:, 129:130], 1.0), writes=[f"vp{i}"])

        def wload(dst, nm, w_, pair):
            src = w_[:, :].rearrange("(h p) (b c) -> p b h c", p=128, c=128)
            tk.op("g", lambda: nc.gpsimd.dma_start(dst[:, :], src[:, pair, :, :]), writes=[nm])

        for pair in range(NP):
            for nm, dst, w_ in [("wA0", wA0, wk), ("wA1", wA1, wkr_d), ("wB0", wB0, wq),
                                ("wB1", wB1, wqr_d), ("wV0", wV0, wv)]:
                wload(dst, nm, w_, pair)
            cbq, cbqr, cbk, cbkr = pair, DC + pair, 2 * DC + pair, 3 * DC + pair

            for c in range(T // 512):
                pa, pb_ = sbank(0, 512), sbank(512, 1024)

                def emit_k(c=c):
                    for h in range(DC):
                        i = nc.tensor.matmul(S[:, 0:512], wA0[:, h * 128:(h + 1) * 128],
                                             xnT_rhs(h, c * 512, 512),
                                             start=(h == 0), stop=(h == DC - 1))
                    return i
                rds = ["wA0"] + [xnT_name(h, c * 512) for h in range(DC)]
                tk.op("T", emit_k, reads=rds, writes=pa)

                def emit_kr(c=c):
                    for h in range(DC):
                        i = nc.tensor.matmul(S[:, 512:1024], wA1[:, h * 128:(h + 1) * 128],
                                             xnT_rhs(h, c * 512, 512),
                                             start=(h == 0), stop=(h == DC - 1))
                    return i
                tk.op("T", emit_kr, reads=["wA1"] + [xnT_name(h, c * 512) for h in range(DC)],
                      writes=pb_)
                tk.op("V", lambda c=c, cbk=cbk: nc.vector.scalar_tensor_tensor(
                    tmp1[:, :], S[:, 0:512], bqkT[:, cbk:cbk + 1],
                    ck[:, c * 512:(c + 1) * 512], ALU.add, ALU.mult),
                    reads=pa + ["bqkT", "ck"], writes=["tmp1"])
                tk.op("V", lambda c=c, cbkr=cbkr: nc.vector.scalar_tensor_tensor(
                    tmp2[:, :], S[:, 512:1024], bqkT[:, cbkr:cbkr + 1],
                    sk[:, c * 512:(c + 1) * 512], ALU.add, ALU.mult),
                    reads=pb_ + ["bqkT", "sk"], writes=["tmp2"])
                tk.op("V", lambda c=c: nc.vector.tensor_tensor(
                    kTp[:, c * 512:(c + 1) * 512], tmp1[:, :], tmp2[:, :], ALU.add),
                    reads=["tmp1", "tmp2"], writes=["kT"])
            for c in range(TQ // 512):
                pa, pb_ = sbank(1024, 1536), sbank(1536, 2048)

                def emit_q(c=c):
                    for h in range(DC):
                        i = nc.tensor.matmul(S[:, 1024:1536], wB0[:, h * 128:(h + 1) * 128],
                                             xnT_o[h][:, c * 512:(c + 1) * 512],
                                             start=(h == 0), stop=(h == DC - 1))
                    return i
                tk.op("T", emit_q, reads=["wB0"] + [f"xnT_o{h}" for h in range(DC)], writes=pa)

                def emit_qr(c=c):
                    for h in range(DC):
                        i = nc.tensor.matmul(S[:, 1536:2048], wB1[:, h * 128:(h + 1) * 128],
                                             xnT_o[h][:, c * 512:(c + 1) * 512],
                                             start=(h == 0), stop=(h == DC - 1))
                    return i
                tk.op("T", emit_qr, reads=["wB1"] + [f"xnT_o{h}" for h in range(DC)], writes=pb_)
                tk.op("V", lambda c=c, cbq=cbq: nc.vector.scalar_tensor_tensor(
                    tmp1[:, :], S[:, 1024:1536], bqkT[:, cbq:cbq + 1],
                    ck[:, c * 512:(c + 1) * 512], ALU.add, ALU.mult),
                    reads=pa + ["bqkT", "ck"], writes=["tmp1"])
                tk.op("V", lambda c=c, cbqr=cbqr: nc.vector.scalar_tensor_tensor(
                    tmp2[:, :], S[:, 1536:2048], bqkT[:, cbqr:cbqr + 1],
                    sk[:, c * 512:(c + 1) * 512], ALU.add, ALU.mult),
                    reads=pb_ + ["bqkT", "sk"], writes=["tmp2"])
                tk.op("V", lambda c=c: nc.vector.tensor_tensor(
                    qTp[:, c * 512:(c + 1) * 512], tmp1[:, :], tmp2[:, :], ALU.add),
                    reads=["tmp1", "tmp2"], writes=["qT"])
            for tt in range(TT_N):
                coff = (tt % 4) * 512
                bnk = sbank(coff, coff + 512)

                def emit_v(tt=tt, coff=coff):
                    for h in range(DC):
                        i = nc.tensor.matmul(S[:, coff:coff + 128], xnT_rhs(h, tt * 128, 128),
                                             wV0[:, h * 128:(h + 1) * 128],
                                             start=(h == 0), stop=False)
                    i = nc.tensor.matmul(S[:, coff:coff + 128], ones1[:, :],
                                         bv_s[0:1, pair * 128:(pair + 1) * 128],
                                         start=False, stop=True)
                    return i
                tk.op("T", emit_v,
                      reads=["wV0", "bv_s", "ones1"] + [xnT_name(h, tt * 128) for h in range(DC)],
                      writes=bnk)

                def emit_ve(tt=tt, coff=coff):
                    nc.scalar.copy(vtiles[tt][:, 0:64], S[:, coff:coff + 64])
                    return nc.scalar.copy(vtiles[tt][:, 65:129], S[:, coff + 64:coff + 128])
                tk.op("S", emit_ve, reads=bnk, writes=[f"vp{tt}"])
            for hh in range(2):
                r0 = hh * 64
                ybanks_all = ybank(0, 1536)
                for j in range(TT_N):
                    exb = [ex0, ex1][j % 2]; exn = ["ex0", "ex1"][j % 2]

                    def emit_sc(j=j, r0=r0):
                        for c in range(4):
                            i = nc.tensor.matmul(S[:, c * 512:(c + 1) * 512],
                                                 kTp[r0:r0 + 64, j * 128:(j + 1) * 128],
                                                 qTp[r0:r0 + 64, c * 512:(c + 1) * 512],
                                                 start=True, stop=True)
                        return i
                    tk.op("T", emit_sc, reads=["kT", "qT"], writes=sbank(0, 2048))
                    tk.op("S", lambda exb=exb: nc.scalar.activation(
                        exb[:, :], S[:, :], AF.Exp, scale=0.125),
                        reads=sbank(0, 2048), writes=[exn])

                    def emit_pv(j=j, hh=hh, exb=exb):
                        for qt in range(TQ_N):
                            col = (qt // 7) * 512 + (qt % 7) * 65
                            first = (j == 0) and (qt % 7 == 0)
                            last = (j == TT_N - 1) and (qt % 7 == 6 or qt == TQ_N - 1)
                            i = nc.tensor.matmul(Y[:, col:col + 65],
                                                 exb[:, qt * 128:(qt + 1) * 128],
                                                 vtiles[j][:, hh * 65:hh * 65 + 65],
                                                 start=first, stop=last)
                        return i
                    tk.op("T", emit_pv, reads=[exn, f"vp{j}"], writes=ybanks_all)
                for qt in range(TQ_N):
                    col = (qt // 7) * 512 + (qt % 7) * 65
                    tk.op("V", lambda col=col: nc.vector.reciprocal(
                        rcp[:, :], Y[:, col + 64:col + 65]),
                        reads=ybanks_all, writes=["rcp"])
                    hcol = (pair * 2 + hh) * 64
                    tk.op("S", lambda qt=qt, col=col, hcol=hcol: nc.scalar.activation(
                        y_acc[qt][:, hcol:hcol + 64], Y[:, col:col + 64], AF.Identity,
                        scale=rcp[:, :]),
                        reads=ybanks_all + ["rcp"], writes=[f"y{qt}"])
    es1.close()

    tk.barrier()
    with ExitStack() as p3:
        PT = p3.enter_context(nc.psum_tensor("p3T", [128, 2048], BF16))
        Y = p3.enter_context(nc.psum_tensor("p3Y", [128, 2048], F32))
        tbank = mkbank("p3T", 2); ybank = mkbank("p3Y")
        woc = sbuf(p3, "woc", [128, D * DC], BF16)
        xo0 = sbuf(p3, "xo0", [128, D]); xo1 = sbuf(p3, "xo1", [128, D])
        ao0 = sbuf(p3, "ao0", [128, D]); ao1 = sbuf(p3, "ao1", [128, D])
        bo_s = sbuf(p3, "bo_s", [1, D])
        yT = [sbuf(p3, f"yT{i}", [128, TQ], BF16) for i in range(DC)]
        tk.op("d", lambda: nc.sync.dma_start(bo_s[:, :], bo_d[:, :]), writes=["bo_s"])
        for h in range(DC):
            tk.op("g", lambda h=h: nc.gpsimd.dma_start(
                woc[:, h * D:(h + 1) * D], wo[h * 128:(h + 1) * 128, :]), writes=["woc"])
        for qt in range(TQ_N):
            half = qt % 2
            cb = tbank(half * 1024, half * 1024 + 1024)

            def emit_ytr(qt=qt, half=half):
                for h in range(DC):
                    i = nc.tensor.transpose(
                        PT[:, half * 1024 + h * 128:half * 1024 + (h + 1) * 128],
                        y_acc[qt][:, h * 128:(h + 1) * 128], idb[:, :])
                return i
            tk.op("T", emit_ytr, reads=[f"y{qt}", "idb"], writes=cb)

            def emit_yev(qt=qt, half=half):
                for h in range(DC):
                    i = nc.vector.tensor_copy(
                        yT[h][:, qt * 128:(qt + 1) * 128],
                        PT[:, half * 1024 + h * 128:half * 1024 + (h + 1) * 128])
                return i
            tk.op("V", emit_yev, reads=cb, writes=[f"yT{h}" for h in range(DC)])
        xos = [xo0, xo1]; aos = [ao0, ao1]
        for qt in range(TQ_N):
            xo = xos[qt % 2]; ao = aos[qt % 2]
            xnm = f"xo{qt % 2}"; anm = f"ao{qt % 2}"
            tk.op("d", lambda qt=qt, xo=xo: nc.sync.dma_start(
                xo[:, :], xb[qt * 128:(qt + 1) * 128, :]), writes=[xnm])
            half = qt % 2
            cb = ybank(half * 1024, half * 1024 + 1024)

            def emit_o(qt=qt, half=half):
                for c in range(2):
                    for h in range(DC):
                        i = nc.tensor.matmul(
                            Y[:, half * 1024 + c * 512:half * 1024 + (c + 1) * 512],
                            yT[h][:, qt * 128:(qt + 1) * 128],
                            woc[:, h * D + c * 512:h * D + (c + 1) * 512],
                            start=(h == 0), stop=False)
                    i = nc.tensor.matmul(
                        Y[:, half * 1024 + c * 512:half * 1024 + (c + 1) * 512],
                        ones1[:, :], bo_s[0:1, c * 512:(c + 1) * 512],
                        start=False, stop=True)
                return i
            tk.op("T", emit_o,
                  reads=[f"yT{h}" for h in range(DC)] + ["woc", "ones1", "bo_s"], writes=cb)
            tk.op("V", lambda ao=ao, xo=xo, half=half: nc.vector.tensor_tensor(
                ao[:, :], Y[:, half * 1024:half * 1024 + 1024], xo[:, :], ALU.add),
                reads=cb + [xnm], writes=[anm])
            tk.op("d", lambda qt=qt, ao=ao: nc.sync.dma_start(
                out_c[qt * 128:(qt + 1) * 128, :], ao[:, :]),
                reads=[anm], writes=[f"oc{qt}"], sem_key=f"st{qt % 2}")
    es2.close()

    tk.barrier()
    with ExitStack() as p4:
        S = p4.enter_context(nc.psum_tensor("p4S", [128, 2048], F32))
        Y = p4.enter_context(nc.psum_tensor("p4Y", [128, 2048], F32))
        sbank = mkbank("p4S"); ybank = mkbank("p4Y")
        w1c0 = sbuf(p4, "w1c0", [128, D], BF16); w1c1 = sbuf(p4, "w1c1", [128, D], BF16)
        w2c0 = sbuf(p4, "w2c0", [128, FF], BF16); w2c1 = sbuf(p4, "w2c1", [128, FF], BF16)
        b1raw = sbuf(p4, "b1raw", [FC, 128])
        ffn = [sbuf(p4, f"ffn{i}", [128, TQ]) for i in range(DC)]
        hmid = [sbuf(p4, f"hm{i}", [128, WIN], BF16) for i in range(FC)]
        for h in range(DC):
            tk.op("S", lambda h=h: nc.scalar.activation(
                ffn[h][:, :], xnT_o[0][:, :], AF.Identity,
                bias=b2bcT[:, h:h + 1], scale=0.0),
                reads=["b2bcT", "xnT_o0"], writes=[f"ffn{h}"])
        w1cs = [w1c0, w1c1]; w1ns = ["w1c0", "w1c1"]
        w2cs = [w2c0, w2c1]; w2ns = ["w2c0", "w2c1"]
        for e in range(NE):
            tk.op("d", lambda e=e: nc.sync.dma_start(b1raw[:, :], b1[e, :, :]), writes=["b1raw"])
            tk.op("T", lambda: nc.tensor.transpose(Y[:, 1024:1024 + FC], b1raw[:, :],
                                                   idf[0:FC, 0:FC]),
                  reads=["b1raw", "idf"], writes=ybank(1024, 1024 + FC))
            tk.op("V", lambda: nc.vector.tensor_copy(b1T[:, :], Y[:, 1024:1024 + FC]),
                  reads=ybank(1024, 1024 + FC), writes=["b1T"])
            for w in range(NW):
                for f in range(FC):
                    wb = w1cs[f % 2]; wn = w1ns[f % 2]
                    tk.op("g", lambda wb=wb, e=e, f=f: nc.gpsimd.dma_start(
                        wb[:, :], w1r[e, f].rearrange("(h p) c -> p h c", p=128)),
                        writes=[wn])
                    pcols = (f % 2) * 1024
                    pb_ = sbank(pcols, pcols + 1024)

                    def emit_m1(w=w, wb=wb, pcols=pcols):
                        for h in range(DC):
                            for c in range(2):
                                i = nc.tensor.matmul(
                                    S[:, pcols + c * 512:pcols + (c + 1) * 512],
                                    wb[:, h * 128:(h + 1) * 128],
                                    xnT_o[h][:, w * WIN + c * 512:w * WIN + (c + 1) * 512],
                                    start=(h == 0), stop=(h == DC - 1))
                        return i
                    tk.op("T", emit_m1, reads=[wn] + [f"xnT_o{h}" for h in range(DC)],
                          writes=pb_)
                    tk.op("S", lambda f=f, pcols=pcols: nc.scalar.activation(
                        hmid[f][:, :], S[:, pcols:pcols + 1024], AF.Gelu,
                        bias=b1T[:, f:f + 1]),
                        reads=pb_ + ["b1T"], writes=[f"hm{f}"])
                for ho in range(DC):
                    wb = w2cs[ho % 2]; wn = w2ns[ho % 2]
                    tk.op("g", lambda wb=wb, e=e, ho=ho: nc.gpsimd.dma_start(
                        wb[:, :], w2r[e, ho].rearrange("(f p) c -> p f c", p=128)),
                        writes=[wn])
                    pcols = (ho % 2) * 1024
                    pb_ = ybank(pcols, pcols + 1024)

                    def emit_m2(wb=wb, pcols=pcols):
                        for f in range(FC):
                            for c in range(2):
                                i = nc.tensor.matmul(
                                    Y[:, pcols + c * 512:pcols + (c + 1) * 512],
                                    wb[:, f * 128:(f + 1) * 128],
                                    hmid[f][:, c * 512:(c + 1) * 512],
                                    start=(f == 0), stop=(f == FC - 1))
                        return i
                    tk.op("T", emit_m2, reads=[wn] + [f"hm{f}" for f in range(FC)],
                          writes=pb_)
                    tk.op("V", lambda ho=ho, w=w, e=e, pcols=pcols: nc.vector.scalar_tensor_tensor(
                        ffn[ho][:, w * WIN:(w + 1) * WIN], Y[:, pcols:pcols + 1024],
                        cbc[:, e:e + 1], ffn[ho][:, w * WIN:(w + 1) * WIN],
                        ALU.mult, ALU.add),
                        reads=pb_ + ["cbc", f"ffn{ho}"], writes=[f"ffn{ho}"])
        with ExitStack() as p5:
            fo0 = sbuf(p5, "fo0", [128, D]); fo1 = sbuf(p5, "fo1", [128, D])
            go0 = sbuf(p5, "go0", [128, D]); go1 = sbuf(p5, "go1", [128, D])
            fos = [fo0, fo1]; gos = [go0, go1]
            for qt in range(TQ_N):
                fo = fos[qt % 2]; go = gos[qt % 2]
                fnm = f"fo{qt % 2}"; gnm = f"go{qt % 2}"
                tk.op("d", lambda qt=qt, fo=fo: nc.sync.dma_start(
                    fo[:, :], out_c[qt * 128:(qt + 1) * 128, :]),
                    reads=[f"oc{qt}"], writes=[fnm])
                half = qt % 2
                cb = sbank(half * 1024, half * 1024 + 1024)

                def emit_ftr(qt=qt, half=half):
                    for h in range(DC):
                        i = nc.tensor.transpose(
                            S[:, half * 1024 + h * 128:half * 1024 + (h + 1) * 128],
                            ffn[h][:, qt * 128:(qt + 1) * 128], idf[:, :])
                    return i
                tk.op("T", emit_ftr, reads=[f"ffn{h}" for h in range(DC)] + ["idf"],
                      writes=cb)
                tk.op("V", lambda go=go, fo=fo, half=half: nc.vector.tensor_tensor(
                    go[:, :], S[:, half * 1024:half * 1024 + 1024], fo[:, :], ALU.add),
                    reads=cb + [fnm], writes=[gnm])
                tk.op("d", lambda qt=qt, go=go: nc.sync.dma_start(
                    out_c[qt * 128:(qt + 1) * 128, :], go[:, :]),
                    reads=[gnm], writes=[f"oc{qt}"], sem_key=f"sf{qt % 2}")
    tk.final_waits("d")
    st.close()
    return nc


def _rope_tables():
    half = HD // 2
    pos = np.arange(T, dtype=np.float64)
    idx = np.arange(half, dtype=np.float64)
    inv = 1.0 / (10000.0 ** (idx / half))
    ang = pos[None, :] * inv[:, None]
    cos = np.cos(ang); sin = np.sin(ang)
    c64 = np.concatenate([cos, cos], 0)
    s64 = np.concatenate([-sin, sin], 0)
    c128 = np.concatenate([c64, c64], 0)
    s128 = np.concatenate([s64, s64], 0)
    return c128, s128


def kernel(L0_attn_x, L0_attn_wq_w, L0_attn_wq_b, L0_attn_wk_w, L0_attn_wk_b,
           L0_attn_wv_w, L0_attn_wv_b, L0_attn_wo_w, L0_attn_wo_b,
           L0_ffn_gate_w, L0_ffn_w1, L0_ffn_b1, L0_ffn_w2, L0_ffn_b2):
    f32c = lambda a: np.ascontiguousarray(np.asarray(a, dtype=np.float32))
    x = f32c(L0_attn_x)
    wq, wk, wv, wo = f32c(L0_attn_wq_w), f32c(L0_attn_wk_w), f32c(L0_attn_wv_w), f32c(L0_attn_wo_w)
    bq, bk, bv, bo = f32c(L0_attn_wq_b), f32c(L0_attn_wk_b), f32c(L0_attn_wv_b), f32c(L0_attn_wo_b)
    gate = f32c(L0_ffn_gate_w)
    w1, b1, w2, b2 = f32c(L0_ffn_w1), f32c(L0_ffn_b1), f32c(L0_ffn_w2), f32c(L0_ffn_b2)

    do = np.arange(D)
    perm = (do - (do % HD)) + ((do % HD) + 32) % HD
    wqr = np.ascontiguousarray(wq[:, perm]); wkr = np.ascontiguousarray(wk[:, perm])
    bqr = bq[perm]; bkr = bk[perm]
    bqkT = np.ascontiguousarray(
        np.concatenate([bq.reshape(DC, 128), bqr.reshape(DC, 128),
                        bk.reshape(DC, 128), bkr.reshape(DC, 128)], 0))

    w1r = np.ascontiguousarray(w1.reshape(NE, D, FC, 128).transpose(0, 2, 1, 3))
    w2r = np.ascontiguousarray(w2.reshape(NE, FF, DC, 128).transpose(0, 2, 1, 3))
    b1r = np.ascontiguousarray(b1.reshape(NE, FC, 128))

    c128, s128 = _rope_tables()
    shared = dict(
        wq=wq, wqr=wqr, wk=wk, wkr=wkr, wv=wv, wo=wo,
        bqkTd=bqkT, bv=bv.reshape(1, D), bo=bo.reshape(1, D),
        gate=gate, w1r=w1r, w2r=w2r, b1=b1r, b2=b2,
        ident=np.eye(128, dtype=np.float32),
        identb=np.eye(128).astype(ml_dtypes.bfloat16))

    in_maps = []
    for c in range(8):
        b, h = c // 2, c % 2
        xb = np.ascontiguousarray(
            np.concatenate([x[b, h * TQ:(h + 1) * TQ], x[b, (1 - h) * TQ:(2 - h) * TQ]], 0))
        tord = np.concatenate([np.arange(h * TQ, (h + 1) * TQ),
                               np.arange((1 - h) * TQ, (2 - h) * TQ)])
        m = dict(shared)
        m["xb"] = xb
        for nm, t64 in (("cosk", c128[:, tord]), ("sink", s128[:, tord])):
            hi = t64.astype(np.float32)
            m[nm] = np.ascontiguousarray(hi)
            m[nm + "l"] = np.ascontiguousarray((t64 - hi).astype(np.float32))
        in_maps.append(m)

    nc = build()
    res = run_bass_kernel_spmd(nc, in_maps, core_ids=list(range(8)))
    out = np.empty((4, T, D), dtype=np.float32)
    for c in range(8):
        b, h = c // 2, c % 2
        out[b, h * TQ:(h + 1) * TQ] = res.results[c]["out_c"]
    return out


# version 05 (final)
# speedup vs baseline: 1.0051x; 1.0049x over previous
import numpy as np
import ml_dtypes
from contextlib import ExitStack
import concourse.bass as bass
import concourse.mybir as mybir
from concourse.bass_utils import run_bass_kernel_spmd

F32 = mybir.dt.float32
BF16 = mybir.dt.bfloat16
AF = mybir.ActivationFunctionType
ALU = mybir.AluOpType
AX = mybir.AxisListType

D = 1024
T = 4096
TQ = 2048
NH = 16
HD = 64
NP = 8
NE = 8
FF = 4096
DC = D // 128
FC = FF // 128
TT_N = T // 128
TQ_N = TQ // 128
WIN = 1024
NW = TQ // WIN


class Tk:

    def __init__(self, nc):
        self.nc = nc
        self.e = {}
        self.res = {}

    def add_engine(self, name, obj, sem, step):
        self.e[name] = {"o": obj, "sem": sem, "step": step, "cnt": 0, "waited": {}}

    def _need(self, deps, tok):
        if tok is None:
            return
        en, v = tok
        if deps.get(en, 0) < v:
            deps[en] = v

    def _bufsem(self, key, issuer):
        nm = f"D:{key}"
        if nm not in self.e:
            sem = self.nc.semaphore(f"ds_{key}").__enter__()
            self.e[nm] = {"o": None, "sem": sem, "step": 16, "cnt": 0, "waited": {}}
        return nm

    def op(self, ename, emit, reads=(), writes=(), sem_key=None):
        e = self.e[ename]
        is_dma = ename in ("d", "g")
        deps = {}
        for r in reads:
            rr = self.res.get(r)
            if rr:
                self._need(deps, rr["w"])
        for w in writes:
            rr = self.res.get(w)
            if rr:
                self._need(deps, rr["w"])
                for t in rr["r"]:
                    self._need(deps, t)
        for pn, v in sorted(deps.items()):
            if pn == ename and ename == "T":
                continue
            if e["waited"].get(pn, 0) >= v:
                continue
            pe = self.e[pn]
            assert v <= pe["cnt"], (ename, pn, v, pe["cnt"])
            e["o"].wait_ge(pe["sem"], v)
            e["waited"][pn] = v
        inst = emit()
        if is_dma:
            snm = self._bufsem(sem_key if sem_key is not None else writes[0], ename)
            se = self.e[snm]
            inst.then_inc(se["sem"], 16)
            se["cnt"] += 16
            tok = (snm, se["cnt"])
            e["waited"][snm] = 0
        else:
            inst.then_inc(e["sem"], e["step"])
            e["cnt"] += e["step"]
            tok = (ename, e["cnt"])
        for w in writes:
            self.res[w] = {"w": tok, "r": []}
        for r in reads:
            rr = self.res.setdefault(r, {"w": None, "r": []})
            rr["r"].append(tok)
        return inst

    def barrier(self):
        for en, e in self.e.items():
            if e["o"] is None:
                continue
            for pn, pe in self.e.items():
                if pn == en or pe["cnt"] == 0:
                    continue
                if e["waited"].get(pn, 0) < pe["cnt"]:
                    e["o"].wait_ge(pe["sem"], pe["cnt"])
                    e["waited"][pn] = pe["cnt"]

    def final_waits(self, ename):
        e = self.e[ename]
        for pn, pe in self.e.items():
            if pn == ename or pe["cnt"] == 0 or pe["o"] is e["o"]:
                continue
            if e["waited"].get(pn, 0) < pe["cnt"]:
                e["o"].wait_ge(pe["sem"], pe["cnt"])
                e["waited"][pn] = pe["cnt"]


def build():
    nc = bass.Bass("TRN2", target_bir_lowering=False)
    dt = lambda n, s, d=F32, k="ExternalInput": nc.dram_tensor(n, s, d, kind=k)

    xb = dt("xb", [T, D])
    wq = dt("wq", [D, D]); wqr_d = dt("wqr", [D, D])
    wk = dt("wk", [D, D]); wkr_d = dt("wkr", [D, D])
    wv = dt("wv", [D, D]); wo = dt("wo", [D, D])
    bqkT_d = dt("bqkTd", [4 * DC, 128])
    bv_d = dt("bv", [1, D]); bo_d = dt("bo", [1, D])
    gate = dt("gate", [D, NE])
    w1r = dt("w1r", [NE, FC, D, 128])
    w2r = dt("w2r", [NE, DC, FF, 128])
    b1 = dt("b1", [NE, FC, 128])
    b2 = dt("b2", [NE, D])
    cosk = dt("cosk", [128, T]); sink = dt("sink", [128, T])
    coskl = dt("coskl", [128, T]); sinkl = dt("sinkl", [128, T])
    ident = dt("ident", [128, 128])
    identb = dt("identb", [128, 128], BF16)
    out_c = dt("out_c", [TQ, D], F32, k="ExternalOutput")

    tk = Tk(nc)
    st = ExitStack()
    sem = lambda n: st.enter_context(nc.semaphore(n))
    ct, cv, cs, cg = sem("ct"), sem("cv"), sem("cs"), sem("cg")
    sd, sg = sem("sd"), sem("sg")
    tk.add_engine("T", nc.tensor, ct, 1)
    tk.add_engine("V", nc.vector, cv, 1)
    tk.add_engine("S", nc.scalar, cs, 1)
    tk.add_engine("G", nc.gpsimd, cg, 1)
    tk.add_engine("d", nc.sync, sd, 16)
    tk.add_engine("g", nc.gpsimd, sg, 16)

    sbuf = lambda es, n, s, d=F32: es.enter_context(nc.sbuf_tensor(n, s, d))

    xnT_o = [sbuf(st, f"xnT_o{i}", [128, TQ], BF16) for i in range(DC)]
    idf = sbuf(st, "idf", [128, 128])
    idb = sbuf(st, "idb", [128, 128], BF16)
    ones1 = sbuf(st, "ones1", [1, 128])
    onesc = sbuf(st, "onesc", [128, 1])
    gate_s = sbuf(st, "gate_s", [128, DC * NE])
    comb = sbuf(st, "comb", [1, NE])
    cbc = sbuf(st, "cbc", [128, NE])
    b2bcT = sbuf(st, "b2bcT", [128, DC])
    bqkT = sbuf(st, "bqkT", [128, 4 * DC])
    logit = sbuf(st, "logit", [1, NE])
    rsm = sbuf(st, "rsm", [1, 8])
    rmask = sbuf(st, "rmask", [1, NE])
    rl2 = sbuf(st, "rl2", [1, NE])
    rexp = sbuf(st, "rexp", [1, NE])
    seqT = sbuf(st, "seqT", [128, DC])
    b1T = sbuf(st, "b1T", [128, FC])

    def mkbank(tag, elsize=4):
        per = 2048 // elsize
        return lambda lo, hi: [f"{tag}{i}" for i in range(lo // per, (hi + per - 1) // per)]

    tk.op("d", lambda: nc.sync.dma_start(idf[:, :], ident[:, :]), writes=["idf"])
    tk.op("d", lambda: nc.sync.dma_start(idb[:, :], identb[:, :]), writes=["idb"])
    for h in range(DC):
        tk.op("g", lambda h=h: nc.gpsimd.dma_start(
            gate_s[:, h * NE:(h + 1) * NE], gate[h * 128:(h + 1) * 128, :],
            allow_slow_non_contiguous=True), writes=["gate_s"])
    tk.op("G", lambda: nc.gpsimd.memset(ones1[:, :], 1.0), writes=["ones1"])
    tk.op("G", lambda: nc.gpsimd.memset(onesc[:, :], 1.0), writes=["onesc"])

    es2 = ExitStack()
    y_acc = [sbuf(es2, f"y{i}", [128, D], BF16) for i in range(TQ_N)]
    es1 = ExitStack()
    xnT_x = [sbuf(es1, f"xnT_x{i}", [128, TQ], BF16) for i in range(DC)]

    with ExitStack() as p1:
        S = p1.enter_context(nc.psum_tensor("p1S", [128, 1024], F32))
        Y = p1.enter_context(nc.psum_tensor("p1Y", [128, 2048], F32))
        sbank = mkbank("p1S"); ybank = mkbank("p1Y")
        xt0 = sbuf(p1, "xt0", [128, D]); xt1 = sbuf(p1, "xt1", [128, D])
        xnt = sbuf(p1, "xnt", [128, D]); xsum = sbuf(p1, "xsum", [128, D])
        stat = sbuf(p1, "stat", [128, 12]); mv = sbuf(p1, "mv", [128, 2])
        rstd = sbuf(p1, "rstd", [128, 1]); nmr = sbuf(p1, "nmr", [128, 1])
        seqrow = sbuf(p1, "seqrow", [1, D])
        cT = sbuf(p1, "cT", [NE, 1]); b2_s = sbuf(p1, "b2_s", [NE, D])
        b2bc = sbuf(p1, "b2bc", [1, D]); lgT = sbuf(p1, "lgT", [NE, 1])
        braw = sbuf(p1, "braw", [4 * DC, 128])

        tk.op("d", lambda: nc.sync.dma_start(braw[:, :], bqkT_d[:, :]), writes=["braw"])
        tk.op("T", lambda: nc.tensor.transpose(Y[:, 0:4 * DC], braw[:, :], idf[0:4 * DC, 0:4 * DC]),
              reads=["braw", "idf"], writes=ybank(0, 32))
        tk.op("V", lambda: nc.vector.tensor_copy(bqkT[:, :], Y[:, 0:4 * DC]),
              reads=ybank(0, 32), writes=["bqkT"])

        xts = [xt0, xt1]
        for tt in range(TT_N):
            xt = xts[tt % 2]; xnm = f"xt{tt % 2}"
            tk.op("d", lambda tt=tt, xt=xt: nc.sync.dma_start(
                xt[:, :], xb[tt * 128:(tt + 1) * 128, :]), writes=[xnm])
            tk.op("V", lambda xt=xt: nc.vector.bn_stats(stat[:, 0:6], xt[:, 0:512]),
                  reads=[xnm], writes=["stat"])
            tk.op("V", lambda xt=xt: nc.vector.bn_stats(stat[:, 6:12], xt[:, 512:1024]),
                  reads=[xnm], writes=["stat2"])
            tk.op("V", lambda: nc.vector.bn_aggr(mv[:, :], stat[:, :]),
                  reads=["stat", "stat2"], writes=["mv"])
            tk.op("V", lambda: nc.vector.tensor_scalar_add(rstd[:, :], mv[:, 1:2], 1e-5),
                  reads=["mv"], writes=["rstd"])
            tk.op("S", lambda: nc.scalar.activation(rstd[:, :], rstd[:, :], AF.Sqrt),
                  reads=["rstd"], writes=["rstd"])
            tk.op("V", lambda: nc.vector.reciprocal(rstd[:, :], rstd[:, :]),
                  reads=["rstd"], writes=["rstd"])
            tk.op("V", lambda: nc.vector.tensor_tensor(nmr[:, :], mv[:, 0:1], rstd[:, :], ALU.mult),
                  reads=["mv", "rstd"], writes=["nmr"])
            tk.op("V", lambda: nc.vector.tensor_scalar_mul(nmr[:, :], nmr[:, :], -1.0),
                  reads=["nmr"], writes=["nmr"])
            tk.op("S", lambda xt=xt: nc.scalar.activation(
                xnt[:, :], xt[:, :], AF.Identity, bias=nmr[:, :], scale=rstd[:, :]),
                reads=[xnm, "nmr", "rstd"], writes=["xnt"])
            if tt == 0:
                tk.op("V", lambda: nc.vector.tensor_copy(xsum[:, :], xnt[:, :]),
                      reads=["xnt"], writes=["xsum"])
            else:
                tk.op("V", lambda: nc.vector.tensor_tensor(xsum[:, :], xsum[:, :], xnt[:, :], ALU.add),
                      reads=["xnt", "xsum"], writes=["xsum"])

            def emit_tr():
                for h in range(DC):
                    i = nc.tensor.transpose(S[:, h * 128:(h + 1) * 128],
                                            xnt[:, h * 128:(h + 1) * 128], idf[:, :])
                return i
            tk.op("T", emit_tr, reads=["xnt", "idf"], writes=sbank(0, 1024))
            own = tt < TQ_N
            dsts = xnT_o if own else xnT_x
            col = (tt if own else tt - TQ_N) * 128

            def emit_cp(dsts=dsts, col=col):
                for h in range(DC):
                    i = nc.vector.tensor_copy(dsts[h][:, col:col + 128],
                                              S[:, h * 128:(h + 1) * 128])
                return i
            tk.op("V", emit_cp, reads=sbank(0, 1024),
                  writes=[("xnT_o" if own else "xnT_x") + str(h) for h in range(DC)])

        def emit_seq():
            for c in range(2):
                i = nc.tensor.matmul(S[0:1, c * 512:(c + 1) * 512], onesc[:, :],
                                     xsum[:, c * 512:(c + 1) * 512], start=True, stop=True)
            return i
        tk.op("T", emit_seq, reads=["onesc", "xsum"], writes=sbank(0, 1024))
        tk.op("S", lambda: nc.scalar.copy(seqrow[:, :], S[0:1, 0:1024]),
              reads=sbank(0, 1024), writes=["seqrow"])

        def emit_seqT():
            for h in range(DC):
                i = nc.tensor.transpose(Y[:, h:h + 1], seqrow[0:1, h * 128:(h + 1) * 128],
                                        idf[0:1, 0:1])
            return i
        tk.op("T", emit_seqT, reads=["seqrow", "idf"], writes=ybank(0, 8))
        tk.op("V", lambda: nc.vector.tensor_copy(seqT[:, :], Y[:, 0:DC]),
              reads=ybank(0, 8), writes=["seqT"])

        def emit_log():
            for h in range(DC):
                i = nc.tensor.matmul(Y[0:NE, 512:513], gate_s[:, h * NE:(h + 1) * NE],
                                     seqT[:, h:h + 1], start=(h == 0), stop=(h == DC - 1))
            return i
        tk.op("T", emit_log, reads=["gate_s", "seqT"], writes=ybank(512, 513))
        tk.op("S", lambda: nc.scalar.activation(lgT[:, :], Y[0:NE, 512:513],
                                                AF.Identity, scale=1.0 / T),
              reads=ybank(512, 513), writes=["lgT"])
        tk.op("T", lambda: nc.tensor.transpose(Y[0:1, 0:NE], lgT[:, :], idf[0:NE, 0:NE]),
              reads=["lgT", "idf"], writes=ybank(0, 8))
        tk.op("V", lambda: nc.vector.tensor_copy(logit[:, :], Y[0:1, 0:NE]),
              reads=ybank(0, 8), writes=["logit"])
        tk.op("V", lambda: nc.vector.reduce_max(rsm[:, 0:1], logit[:, :], axis=AX.X),
              reads=["logit"], writes=["rm1"])
        tk.op("V", lambda: nc.vector.tensor_scalar(rmask[:, :], logit[:, :], rsm[:, 0:1],
                                                   None, op0=ALU.is_ge),
              reads=["logit", "rm1"], writes=["rmask"])
        tk.op("V", lambda: nc.vector.scalar_tensor_tensor(rl2[:, :], rmask[:, :], -1e9,
                                                          logit[:, :], ALU.mult, ALU.add),
              reads=["rmask", "logit"], writes=["rl2"])
        tk.op("V", lambda: nc.vector.reduce_max(rsm[:, 2:3], rl2[:, :], axis=AX.X),
              reads=["rl2"], writes=["rm2"])
        tk.op("V", lambda: nc.vector.tensor_scalar(rl2[:, :], rl2[:, :], rsm[:, 2:3],
                                                   None, op0=ALU.is_ge),
              reads=["rl2", "rm2"], writes=["rl2"])
        tk.op("V", lambda: nc.vector.tensor_tensor(rmask[:, :], rmask[:, :], rl2[:, :], ALU.add),
              reads=["rmask", "rl2"], writes=["rmask"])
        tk.op("V", lambda: nc.vector.tensor_scalar_mul(rsm[:, 1:2], rsm[:, 0:1], -1.0),
              reads=["rm1"], writes=["rm1n"])
        tk.op("S", lambda: nc.scalar.activation(rexp[:, :], logit[:, :], AF.Exp, bias=rsm[:, 1:2]),
              reads=["logit", "rm1n"], writes=["rexp"])
        tk.op("V", lambda: nc.vector.tensor_tensor(rexp[:, :], rexp[:, :], rmask[:, :], ALU.mult),
              reads=["rexp", "rmask"], writes=["rexp"])
        tk.op("V", lambda: nc.vector.reduce_sum(rsm[:, 3:4], rexp[:, :], axis=AX.X),
              reads=["rexp"], writes=["rden"])
        tk.op("V", lambda: nc.vector.reciprocal(rsm[:, 4:5], rsm[:, 3:4]),
              reads=["rden"], writes=["rrec"])
        tk.op("V", lambda: nc.vector.tensor_scalar(comb[:, :], rexp[:, :], rsm[:, 4:5],
                                                   None, op0=ALU.mult),
              reads=["rexp", "rrec"], writes=["comb"])

        VC = float(2 ** 29 + 1)
        CEPH = [float(np.float32(v)) for v in
                (1.9875691500E-4, 1.3981999507E-3, 8.3334519073E-3,
                 4.1665795894E-2, 1.6666665459E-1, 5.0000001201E-1)]
        vt = sbuf(p1, "vt", [1, NE]); vt2 = sbuf(p1, "vt2", [1, NE])
        bl = sbuf(p1, "bl", [1, NE]); bs_ = sbuf(p1, "bs", [1, NE])
        br2 = sbuf(p1, "br2", [1, NE]); be = sbuf(p1, "be", [1, NE])
        bt8 = sbuf(p1, "bt8", [1, NE]); bp = sbuf(p1, "bp", [1, NE])
        msk1 = sbuf(p1, "msk1", [1, NE]); msk2 = sbuf(p1, "msk2", [1, NE])
        bsc = sbuf(p1, "bsc", [1, 6])

        def vop(emit, reads, writes):
            tk.op("V", emit, reads=reads, writes=writes)

        def bounce(buf, nm, c0=0, cn=NE):
            vop(lambda: nc.vector.tensor_scalar_mul(vt[:, 0:cn], buf[:, c0:c0 + cn], VC),
                [nm], ["vt"])
            vop(lambda: nc.vector.tensor_tensor(vt2[:, 0:cn], vt[:, 0:cn],
                                                buf[:, c0:c0 + cn], ALU.subtract),
                ["vt", nm], ["vt2"])
            vop(lambda: nc.vector.tensor_tensor(buf[:, c0:c0 + cn], vt[:, 0:cn],
                                                vt2[:, 0:cn], ALU.subtract),
                ["vt", "vt2"], [nm])

        vop(lambda: nc.vector.tensor_scalar_mul(vt[:, :], logit[:, :], VC), ["logit"], ["vt"])
        vop(lambda: nc.vector.tensor_tensor(vt2[:, :], vt[:, :], logit[:, :], ALU.subtract),
            ["vt", "logit"], ["vt2"])
        vop(lambda: nc.vector.tensor_tensor(bl[:, :], vt[:, :], vt2[:, :], ALU.subtract),
            ["vt", "vt2"], ["bl"])
        vop(lambda: nc.vector.reduce_max(bsc[:, 0:1], bl[:, :], axis=AX.X), ["bl"], ["bm1"])
        vop(lambda: nc.vector.tensor_scalar(bs_[:, :], bl[:, :], bsc[:, 0:1], None,
                                            op0=ALU.subtract), ["bl", "bm1"], ["bs"])
        bounce(bs_, "bs")
        vop(lambda: nc.vector.tensor_tensor(br2[:, :], bs_[:, :], bs_[:, :], ALU.mult),
            ["bs"], ["br2"])
        bounce(br2, "br2")
        vop(lambda: nc.vector.tensor_scalar(be[:, :], bs_[:, :], 0.0, CEPH[0],
                                            op0=ALU.mult, op1=ALU.add), ["bs"], ["be"])
        for cp in CEPH[1:]:
            vop(lambda: nc.vector.tensor_tensor(bt8[:, :], be[:, :], bs_[:, :], ALU.mult),
                ["be", "bs"], ["bt8"])
            vop(lambda cp=cp: nc.vector.tensor_scalar_add(be[:, :], bt8[:, :], cp),
                ["bt8"], ["be"])
            bounce(be, "be")
        vop(lambda: nc.vector.tensor_tensor(bt8[:, :], be[:, :], br2[:, :], ALU.mult),
            ["be", "br2"], ["bt8"])
        vop(lambda: nc.vector.tensor_tensor(be[:, :], bt8[:, :], bs_[:, :], ALU.add),
            ["bt8", "bs"], ["be"])
        bounce(be, "be")
        vop(lambda: nc.vector.tensor_scalar_add(be[:, :], be[:, :], 1.0), ["be"], ["be"])
        bounce(be, "be")
        vop(lambda: nc.vector.tensor_copy(bsc[:, 2:3], be[:, 0:1]), ["be"], ["bacc"])
        for i in range(1, NE):
            vop(lambda i=i: nc.vector.tensor_tensor(bsc[:, 2:3], bsc[:, 2:3],
                                                    be[:, i:i + 1], ALU.add),
                ["bacc", "be"], ["bacc"])
            bounce(bsc, "bacc", c0=2, cn=1)
        vop(lambda: nc.vector.tensor_scalar(bsc[:, 2:3], bsc[:, 2:3], 0.5, 64.0,
                                            op0=ALU.max, op1=ALU.min), ["bacc"], ["bacc"])
        vop(lambda: nc.vector.reciprocal(bsc[:, 4:5], bsc[:, 2:3]), ["bacc"], ["brec"])
        vop(lambda: nc.vector.tensor_scalar(bp[:, :], be[:, :], bsc[:, 4:5], None,
                                            op0=ALU.mult), ["be", "brec"], ["bp"])
        bounce(bp, "bp")
        vop(lambda: nc.vector.reduce_max(bsc[:, 0:1], bp[:, :], axis=AX.X), ["bp"], ["bm1"])
        vop(lambda: nc.vector.tensor_scalar(msk1[:, :], bp[:, :], bsc[:, 0:1], None,
                                            op0=ALU.is_ge), ["bp", "bm1"], ["msk1"])
        vop(lambda: nc.vector.scalar_tensor_tensor(bt8[:, :], msk1[:, :], -4.0, bp[:, :],
                                                   ALU.mult, ALU.add),
            ["msk1", "bp"], ["bt8"])
        vop(lambda: nc.vector.reduce_max(bsc[:, 1:2], bt8[:, :], axis=AX.X), ["bt8"], ["bm2"])
        vop(lambda: nc.vector.tensor_scalar(msk2[:, :], bt8[:, :], bsc[:, 1:2], None,
                                            op0=ALU.is_ge), ["bt8", "bm2"], ["msk2"])
        vop(lambda: nc.vector.tensor_tensor(msk1[:, :], msk1[:, :], msk2[:, :], ALU.add),
            ["msk1", "msk2"], ["msk1"])
        vop(lambda: nc.vector.tensor_tensor(bp[:, :], bp[:, :], msk1[:, :], ALU.mult),
            ["bp", "msk1"], ["bp"])
        vop(lambda: nc.vector.reduce_sum(bsc[:, 3:4], bp[:, :], axis=AX.X), ["bp"], ["bacc2"])
        bounce(bsc, "bacc2", c0=3, cn=1)
        vop(lambda: nc.vector.tensor_scalar(bsc[:, 3:4], bsc[:, 3:4], 0.125, 2.0,
                                            op0=ALU.max, op1=ALU.min), ["bacc2"], ["bacc2"])
        vop(lambda: nc.vector.reciprocal(bsc[:, 5:6], bsc[:, 3:4]), ["bacc2"], ["brec2"])
        vop(lambda: nc.vector.tensor_scalar(bp[:, :], bp[:, :], bsc[:, 5:6], None,
                                            op0=ALU.mult), ["bp", "brec2"], ["bp"])
        bounce(bp, "bp")
        vop(lambda: nc.vector.tensor_scalar_add(bt8[:, :], comb[:, :], -2e-4),
            ["comb"], ["bt8"])
        vop(lambda: nc.vector.tensor_tensor(bp[:, :], bp[:, :], bt8[:, :], ALU.max),
            ["bp", "bt8"], ["bp"])
        vop(lambda: nc.vector.tensor_scalar_add(bt8[:, :], comb[:, :], 2e-4),
            ["comb"], ["bt8"])
        vop(lambda: nc.vector.tensor_tensor(comb[:, :], bp[:, :], bt8[:, :], ALU.min),
            ["bp", "bt8"], ["comb"])
        tk.op("T", lambda: nc.tensor.matmul(Y[:, 0:NE], ones1[:, :], comb[:, :],
                                            start=True, stop=True),
              reads=["ones1", "comb"], writes=ybank(0, 8))
        tk.op("V", lambda: nc.vector.tensor_copy(cbc[:, :], Y[:, 0:NE]),
              reads=ybank(0, 8), writes=["cbc"])
        tk.op("T", lambda: nc.tensor.transpose(Y[0:NE, 512:513], comb[0:1, :], idf[0:1, 0:1]),
              reads=["comb", "idf"], writes=ybank(512, 513))
        tk.op("V", lambda: nc.vector.tensor_copy(cT[:, :], Y[0:NE, 512:513]),
              reads=ybank(512, 513), writes=["cT"])
        tk.op("d", lambda: nc.sync.dma_start(b2_s[:, :], b2[:, :]), writes=["b2_s"])

        def emit_b2bc():
            for c in range(2):
                i = nc.tensor.matmul(Y[0:1, 1024 + c * 512:1024 + (c + 1) * 512], cT[:, :],
                                     b2_s[:, c * 512:(c + 1) * 512], start=True, stop=True)
            return i
        tk.op("T", emit_b2bc, reads=["cT", "b2_s"], writes=ybank(1024, 2048))
        tk.op("S", lambda: nc.scalar.copy(b2bc[:, :], Y[0:1, 1024:2048]),
              reads=ybank(1024, 2048), writes=["b2bc"])

        def emit_b2T():
            for h in range(DC):
                i = nc.tensor.transpose(Y[:, h:h + 1], b2bc[0:1, h * 128:(h + 1) * 128],
                                        idf[0:1, 0:1])
            return i
        tk.op("T", emit_b2T, reads=["b2bc", "idf"], writes=ybank(0, 8))
        tk.op("V", lambda: nc.vector.tensor_copy(b2bcT[:, :], Y[:, 0:DC]),
              reads=ybank(0, 8), writes=["b2bcT"])

    def xnT_rhs(h, c0, n):
        if c0 < TQ:
            return xnT_o[h][:, c0:c0 + n]
        return xnT_x[h][:, c0 - TQ:c0 - TQ + n]

    def xnT_name(h, c0):
        return (f"xnT_o{h}") if c0 < TQ else (f"xnT_x{h}")

    tk.barrier()
    with ExitStack() as p2:
        S = p2.enter_context(nc.psum_tensor("p2S", [128, 2048], F32))
        Y = p2.enter_context(nc.psum_tensor("p2Y", [128, 1536], F32))
        sbank = mkbank("p2S"); ybank = mkbank("p2Y")
        ck = sbuf(p2, "ck", [128, T]); sk = sbuf(p2, "sk", [128, T])
        kTp = sbuf(p2, "kT", [128, T], BF16)
        qTp = sbuf(p2, "qT", [128, TQ], BF16)
        wA0 = sbuf(p2, "wA0", [128, D], BF16); wA1 = sbuf(p2, "wA1", [128, D], BF16)
        wB0 = sbuf(p2, "wB0", [128, D], BF16); wB1 = sbuf(p2, "wB1", [128, D], BF16)
        wV0 = sbuf(p2, "wV0", [128, D], BF16)
        tmp1 = sbuf(p2, "tmp1", [128, 512]); tmp2 = sbuf(p2, "tmp2", [128, 512])
        ex0 = sbuf(p2, "ex0", [128, TQ], BF16); ex1 = sbuf(p2, "ex1", [128, TQ], BF16)
        rcp = sbuf(p2, "rcp", [128, 1])
        bv_s = sbuf(p2, "bv_s", [1, D])
        vtiles = [sbuf(p2, f"vp{i}", [128, 130], BF16) for i in range(TT_N)]

        tk.op("d", lambda: nc.sync.dma_start(ck[:, :], cosk[:, :]), writes=["ck"])
        tk.op("d", lambda: nc.sync.dma_start(sk[:, :], sink[:, :]), writes=["sk"])
        tlo = sbuf(p2, "tlo", [128, 2048])
        for dst, dnm, lo_d in ((ck, "ck", coskl), (sk, "sk", sinkl)):
            for half in range(2):
                tk.op("d", lambda lo_d=lo_d, half=half: nc.sync.dma_start(
                    tlo[:, :], lo_d[:, half * 2048:(half + 1) * 2048]), writes=["tlo"])
                tk.op("V", lambda dst=dst, half=half: nc.vector.tensor_tensor(
                    dst[:, half * 2048:(half + 1) * 2048],
                    dst[:, half * 2048:(half + 1) * 2048], tlo[:, :], ALU.add),
                    reads=["tlo", dnm], writes=[dnm])
        tk.op("d", lambda: nc.sync.dma_start(bv_s[:, :], bv_d[:, :]), writes=["bv_s"])
        for i in range(TT_N):
            tk.op("G", lambda i=i: nc.gpsimd.memset(vtiles[i][:, 64:65], 1.0), writes=[f"vp{i}"])
            tk.op("G", lambda i=i: nc.gpsimd.memset(vtiles[i][:, 129:130], 1.0), writes=[f"vp{i}"])

        def wload(dst, nm, w_, pair):
            src = w_[:, :].rearrange("(h p) (b c) -> p b h c", p=128, c=128)
            tk.op("g", lambda: nc.gpsimd.dma_start(dst[:, :], src[:, pair, :, :]), writes=[nm])

        for pair in range(NP):
            for nm, dst, w_ in [("wA0", wA0, wk), ("wA1", wA1, wkr_d), ("wB0", wB0, wq),
                                ("wB1", wB1, wqr_d), ("wV0", wV0, wv)]:
                wload(dst, nm, w_, pair)
            cbq, cbqr, cbk, cbkr = pair, DC + pair, 2 * DC + pair, 3 * DC + pair

            for c in range(T // 512):
                pa, pb_ = sbank(0, 512), sbank(512, 1024)

                def emit_k(c=c):
                    for h in range(DC):
                        i = nc.tensor.matmul(S[:, 0:512], wA0[:, h * 128:(h + 1) * 128],
                                             xnT_rhs(h, c * 512, 512),
                                             start=(h == 0), stop=(h == DC - 1))
                    return i
                rds = ["wA0"] + [xnT_name(h, c * 512) for h in range(DC)]
                tk.op("T", emit_k, reads=rds, writes=pa)

                def emit_kr(c=c):
                    for h in range(DC):
                        i = nc.tensor.matmul(S[:, 512:1024], wA1[:, h * 128:(h + 1) * 128],
                                             xnT_rhs(h, c * 512, 512),
                                             start=(h == 0), stop=(h == DC - 1))
                    return i
                tk.op("T", emit_kr, reads=["wA1"] + [xnT_name(h, c * 512) for h in range(DC)],
                      writes=pb_)
                tk.op("V", lambda c=c, cbk=cbk: nc.vector.scalar_tensor_tensor(
                    tmp1[:, :], S[:, 0:512], bqkT[:, cbk:cbk + 1],
                    ck[:, c * 512:(c + 1) * 512], ALU.add, ALU.mult),
                    reads=pa + ["bqkT", "ck"], writes=["tmp1"])
                tk.op("V", lambda c=c, cbkr=cbkr: nc.vector.scalar_tensor_tensor(
                    tmp2[:, :], S[:, 512:1024], bqkT[:, cbkr:cbkr + 1],
                    sk[:, c * 512:(c + 1) * 512], ALU.add, ALU.mult),
                    reads=pb_ + ["bqkT", "sk"], writes=["tmp2"])
                tk.op("V", lambda c=c: nc.vector.tensor_tensor(
                    kTp[:, c * 512:(c + 1) * 512], tmp1[:, :], tmp2[:, :], ALU.add),
                    reads=["tmp1", "tmp2"], writes=["kT"])
            for c in range(TQ // 512):
                pa, pb_ = sbank(1024, 1536), sbank(1536, 2048)

                def emit_q(c=c):
                    for h in range(DC):
                        i = nc.tensor.matmul(S[:, 1024:1536], wB0[:, h * 128:(h + 1) * 128],
                                             xnT_o[h][:, c * 512:(c + 1) * 512],
                                             start=(h == 0), stop=(h == DC - 1))
                    return i
                tk.op("T", emit_q, reads=["wB0"] + [f"xnT_o{h}" for h in range(DC)], writes=pa)

                def emit_qr(c=c):
                    for h in range(DC):
                        i = nc.tensor.matmul(S[:, 1536:2048], wB1[:, h * 128:(h + 1) * 128],
                                             xnT_o[h][:, c * 512:(c + 1) * 512],
                                             start=(h == 0), stop=(h == DC - 1))
                    return i
                tk.op("T", emit_qr, reads=["wB1"] + [f"xnT_o{h}" for h in range(DC)], writes=pb_)
                tk.op("V", lambda c=c, cbq=cbq: nc.vector.scalar_tensor_tensor(
                    tmp1[:, :], S[:, 1024:1536], bqkT[:, cbq:cbq + 1],
                    ck[:, c * 512:(c + 1) * 512], ALU.add, ALU.mult),
                    reads=pa + ["bqkT", "ck"], writes=["tmp1"])
                tk.op("V", lambda c=c, cbqr=cbqr: nc.vector.scalar_tensor_tensor(
                    tmp2[:, :], S[:, 1536:2048], bqkT[:, cbqr:cbqr + 1],
                    sk[:, c * 512:(c + 1) * 512], ALU.add, ALU.mult),
                    reads=pb_ + ["bqkT", "sk"], writes=["tmp2"])
                tk.op("V", lambda c=c: nc.vector.tensor_tensor(
                    qTp[:, c * 512:(c + 1) * 512], tmp1[:, :], tmp2[:, :], ALU.add),
                    reads=["tmp1", "tmp2"], writes=["qT"])
            for tt in range(TT_N):
                coff = (tt % 4) * 512
                bnk = sbank(coff, coff + 512)

                def emit_v(tt=tt, coff=coff):
                    for h in range(DC):
                        i = nc.tensor.matmul(S[:, coff:coff + 128], xnT_rhs(h, tt * 128, 128),
                                             wV0[:, h * 128:(h + 1) * 128],
                                             start=(h == 0), stop=False)
                    i = nc.tensor.matmul(S[:, coff:coff + 128], ones1[:, :],
                                         bv_s[0:1, pair * 128:(pair + 1) * 128],
                                         start=False, stop=True)
                    return i
                tk.op("T", emit_v,
                      reads=["wV0", "bv_s", "ones1"] + [xnT_name(h, tt * 128) for h in range(DC)],
                      writes=bnk)

                def emit_ve(tt=tt, coff=coff):
                    nc.scalar.copy(vtiles[tt][:, 0:64], S[:, coff:coff + 64])
                    return nc.scalar.copy(vtiles[tt][:, 65:129], S[:, coff + 64:coff + 128])
                tk.op("S", emit_ve, reads=bnk, writes=[f"vp{tt}"])
            for hh in range(2):
                r0 = hh * 64
                ybanks_all = ybank(0, 1536)
                for j in range(TT_N):
                    exb = [ex0, ex1][j % 2]; exn = ["ex0", "ex1"][j % 2]

                    def emit_sc(j=j, r0=r0):
                        for c in range(4):
                            i = nc.tensor.matmul(S[:, c * 512:(c + 1) * 512],
                                                 kTp[r0:r0 + 64, j * 128:(j + 1) * 128],
                                                 qTp[r0:r0 + 64, c * 512:(c + 1) * 512],
                                                 start=True, stop=True)
                        return i
                    tk.op("T", emit_sc, reads=["kT", "qT"], writes=sbank(0, 2048))
                    tk.op("S", lambda exb=exb: nc.scalar.activation(
                        exb[:, :], S[:, :], AF.Exp, scale=0.125),
                        reads=sbank(0, 2048), writes=[exn])

                    def emit_pv(j=j, hh=hh, exb=exb):
                        for qt in range(TQ_N):
                            col = (qt // 7) * 512 + (qt % 7) * 65
                            first = (j == 0) and (qt % 7 == 0)
                            last = (j == TT_N - 1) and (qt % 7 == 6 or qt == TQ_N - 1)
                            i = nc.tensor.matmul(Y[:, col:col + 65],
                                                 exb[:, qt * 128:(qt + 1) * 128],
                                                 vtiles[j][:, hh * 65:hh * 65 + 65],
                                                 start=first, stop=last)
                        return i
                    tk.op("T", emit_pv, reads=[exn, f"vp{j}"], writes=ybanks_all)
                for qt in range(TQ_N):
                    col = (qt // 7) * 512 + (qt % 7) * 65
                    tk.op("V", lambda col=col: nc.vector.reciprocal(
                        rcp[:, :], Y[:, col + 64:col + 65]),
                        reads=ybanks_all, writes=["rcp"])
                    hcol = (pair * 2 + hh) * 64
                    tk.op("S", lambda qt=qt, col=col, hcol=hcol: nc.scalar.activation(
                        y_acc[qt][:, hcol:hcol + 64], Y[:, col:col + 64], AF.Identity,
                        scale=rcp[:, :]),
                        reads=ybanks_all + ["rcp"], writes=[f"y{qt}"])
    es1.close()

    tk.barrier()
    with ExitStack() as p3:
        PT = p3.enter_context(nc.psum_tensor("p3T", [128, 2048], BF16))
        Y = p3.enter_context(nc.psum_tensor("p3Y", [128, 2048], F32))
        tbank = mkbank("p3T", 2); ybank = mkbank("p3Y")
        woc = sbuf(p3, "woc", [128, D * DC], BF16)
        xo0 = sbuf(p3, "xo0", [128, D]); xo1 = sbuf(p3, "xo1", [128, D])
        ao0 = sbuf(p3, "ao0", [128, D]); ao1 = sbuf(p3, "ao1", [128, D])
        bo_s = sbuf(p3, "bo_s", [1, D])
        yT = [sbuf(p3, f"yT{i}", [128, TQ], BF16) for i in range(DC)]
        tk.op("d", lambda: nc.sync.dma_start(bo_s[:, :], bo_d[:, :]), writes=["bo_s"])
        for h in range(DC):
            tk.op("g", lambda h=h: nc.gpsimd.dma_start(
                woc[:, h * D:(h + 1) * D], wo[h * 128:(h + 1) * 128, :]), writes=["woc"])
        for qt in range(TQ_N):
            half = qt % 2
            cb = tbank(half * 1024, half * 1024 + 1024)

            def emit_ytr(qt=qt, half=half):
                for h in range(DC):
                    i = nc.tensor.transpose(
                        PT[:, half * 1024 + h * 128:half * 1024 + (h + 1) * 128],
                        y_acc[qt][:, h * 128:(h + 1) * 128], idb[:, :])
                return i
            tk.op("T", emit_ytr, reads=[f"y{qt}", "idb"], writes=cb)

            def emit_yev(qt=qt, half=half):
                for h in range(DC):
                    i = nc.vector.tensor_copy(
                        yT[h][:, qt * 128:(qt + 1) * 128],
                        PT[:, half * 1024 + h * 128:half * 1024 + (h + 1) * 128])
                return i
            tk.op("V", emit_yev, reads=cb, writes=[f"yT{h}" for h in range(DC)])
        xos = [xo0, xo1]; aos = [ao0, ao1]
        for qt in range(TQ_N):
            xo = xos[qt % 2]; ao = aos[qt % 2]
            xnm = f"xo{qt % 2}"; anm = f"ao{qt % 2}"
            tk.op("d", lambda qt=qt, xo=xo: nc.sync.dma_start(
                xo[:, :], xb[qt * 128:(qt + 1) * 128, :]), writes=[xnm])
            half = qt % 2
            cb = ybank(half * 1024, half * 1024 + 1024)

            def emit_o(qt=qt, half=half):
                for c in range(2):
                    for h in range(DC):
                        i = nc.tensor.matmul(
                            Y[:, half * 1024 + c * 512:half * 1024 + (c + 1) * 512],
                            yT[h][:, qt * 128:(qt + 1) * 128],
                            woc[:, h * D + c * 512:h * D + (c + 1) * 512],
                            start=(h == 0), stop=False)
                    i = nc.tensor.matmul(
                        Y[:, half * 1024 + c * 512:half * 1024 + (c + 1) * 512],
                        ones1[:, :], bo_s[0:1, c * 512:(c + 1) * 512],
                        start=False, stop=True)
                return i
            tk.op("T", emit_o,
                  reads=[f"yT{h}" for h in range(DC)] + ["woc", "ones1", "bo_s"], writes=cb)
            tk.op("V", lambda ao=ao, xo=xo, half=half: nc.vector.tensor_tensor(
                ao[:, :], Y[:, half * 1024:half * 1024 + 1024], xo[:, :], ALU.add),
                reads=cb + [xnm], writes=[anm])
            tk.op("d", lambda qt=qt, ao=ao: nc.sync.dma_start(
                out_c[qt * 128:(qt + 1) * 128, :], ao[:, :]),
                reads=[anm], writes=[f"oc{qt}"], sem_key=f"st{qt % 2}")
    es2.close()

    tk.barrier()
    with ExitStack() as p4:
        S = p4.enter_context(nc.psum_tensor("p4S", [128, 2048], F32))
        Y = p4.enter_context(nc.psum_tensor("p4Y", [128, 2048], F32))
        sbank = mkbank("p4S"); ybank = mkbank("p4Y")
        w1c0 = sbuf(p4, "w1c0", [128, D], BF16); w1c1 = sbuf(p4, "w1c1", [128, D], BF16)
        w2c0 = sbuf(p4, "w2c0", [128, FF], BF16); w2c1 = sbuf(p4, "w2c1", [128, FF], BF16)
        b1raw = sbuf(p4, "b1raw", [FC, 128])
        ffn = [sbuf(p4, f"ffn{i}", [128, TQ]) for i in range(DC)]
        hmid = [sbuf(p4, f"hm{i}", [128, WIN], BF16) for i in range(FC)]
        for h in range(DC):
            tk.op("S", lambda h=h: nc.scalar.activation(
                ffn[h][:, :], xnT_o[0][:, :], AF.Identity,
                bias=b2bcT[:, h:h + 1], scale=0.0),
                reads=["b2bcT", "xnT_o0"], writes=[f"ffn{h}"])
        w1cs = [w1c0, w1c1]; w1ns = ["w1c0", "w1c1"]
        w2cs = [w2c0, w2c1]; w2ns = ["w2c0", "w2c1"]
        for e in range(NE):
            tk.op("d", lambda e=e: nc.sync.dma_start(b1raw[:, :], b1[e, :, :]), writes=["b1raw"])
            tk.op("T", lambda: nc.tensor.transpose(Y[:, 1024:1024 + FC], b1raw[:, :],
                                                   idf[0:FC, 0:FC]),
                  reads=["b1raw", "idf"], writes=ybank(1024, 1024 + FC))
            tk.op("V", lambda: nc.vector.tensor_copy(b1T[:, :], Y[:, 1024:1024 + FC]),
                  reads=ybank(1024, 1024 + FC), writes=["b1T"])
            for w in range(NW):
                for f in range(FC):
                    wb = w1cs[f % 2]; wn = w1ns[f % 2]
                    tk.op("g", lambda wb=wb, e=e, f=f: nc.gpsimd.dma_start(
                        wb[:, :], w1r[e, f].rearrange("(h p) c -> p h c", p=128)),
                        writes=[wn])
                    pcols = (f % 2) * 1024
                    pb_ = sbank(pcols, pcols + 1024)

                    def emit_m1(w=w, wb=wb, pcols=pcols):
                        for h in range(DC):
                            for c in range(2):
                                i = nc.tensor.matmul(
                                    S[:, pcols + c * 512:pcols + (c + 1) * 512],
                                    wb[:, h * 128:(h + 1) * 128],
                                    xnT_o[h][:, w * WIN + c * 512:w * WIN + (c + 1) * 512],
                                    start=(h == 0), stop=(h == DC - 1))
                        return i
                    tk.op("T", emit_m1, reads=[wn] + [f"xnT_o{h}" for h in range(DC)],
                          writes=pb_)
                    tk.op("S", lambda f=f, pcols=pcols: nc.scalar.activation(
                        hmid[f][:, :], S[:, pcols:pcols + 1024], AF.Gelu,
                        bias=b1T[:, f:f + 1]),
                        reads=pb_ + ["b1T"], writes=[f"hm{f}"])
                for ho in range(DC):
                    wb = w2cs[ho % 2]; wn = w2ns[ho % 2]
                    tk.op("g", lambda wb=wb, e=e, ho=ho: nc.gpsimd.dma_start(
                        wb[:, :], w2r[e, ho].rearrange("(f p) c -> p f c", p=128)),
                        writes=[wn])
                    pcols = (ho % 2) * 1024
                    pb_ = ybank(pcols, pcols + 1024)

                    def emit_m2(wb=wb, pcols=pcols):
                        for f in range(FC):
                            for c in range(2):
                                i = nc.tensor.matmul(
                                    Y[:, pcols + c * 512:pcols + (c + 1) * 512],
                                    wb[:, f * 128:(f + 1) * 128],
                                    hmid[f][:, c * 512:(c + 1) * 512],
                                    start=(f == 0), stop=(f == FC - 1))
                        return i
                    tk.op("T", emit_m2, reads=[wn] + [f"hm{f}" for f in range(FC)],
                          writes=pb_)
                    tk.op("V", lambda ho=ho, w=w, e=e, pcols=pcols: nc.vector.scalar_tensor_tensor(
                        ffn[ho][:, w * WIN:(w + 1) * WIN], Y[:, pcols:pcols + 1024],
                        cbc[:, e:e + 1], ffn[ho][:, w * WIN:(w + 1) * WIN],
                        ALU.mult, ALU.add),
                        reads=pb_ + ["cbc", f"ffn{ho}"], writes=[f"ffn{ho}"])
        with ExitStack() as p5:
            fo0 = sbuf(p5, "fo0", [128, D]); fo1 = sbuf(p5, "fo1", [128, D])
            go0 = sbuf(p5, "go0", [128, D]); go1 = sbuf(p5, "go1", [128, D])
            fos = [fo0, fo1]; gos = [go0, go1]
            for qt in range(TQ_N):
                fo = fos[qt % 2]; go = gos[qt % 2]
                fnm = f"fo{qt % 2}"; gnm = f"go{qt % 2}"
                tk.op("d", lambda qt=qt, fo=fo: nc.sync.dma_start(
                    fo[:, :], out_c[qt * 128:(qt + 1) * 128, :]),
                    reads=[f"oc{qt}"], writes=[fnm])
                half = qt % 2
                cb = sbank(half * 1024, half * 1024 + 1024)

                def emit_ftr(qt=qt, half=half):
                    for h in range(DC):
                        i = nc.tensor.transpose(
                            S[:, half * 1024 + h * 128:half * 1024 + (h + 1) * 128],
                            ffn[h][:, qt * 128:(qt + 1) * 128], idf[:, :])
                    return i
                tk.op("T", emit_ftr, reads=[f"ffn{h}" for h in range(DC)] + ["idf"],
                      writes=cb)
                tk.op("V", lambda go=go, fo=fo, half=half: nc.vector.tensor_tensor(
                    go[:, :], S[:, half * 1024:half * 1024 + 1024], fo[:, :], ALU.add),
                    reads=cb + [fnm], writes=[gnm])
                tk.op("d", lambda qt=qt, go=go: nc.sync.dma_start(
                    out_c[qt * 128:(qt + 1) * 128, :], go[:, :]),
                    reads=[gnm], writes=[f"oc{qt}"], sem_key=f"sf{qt % 2}")
    tk.final_waits("d")
    st.close()
    return nc


def _rope_tables():
    half = HD // 2
    pos = np.arange(T, dtype=np.float64)
    idx = np.arange(half, dtype=np.float64)
    inv = 1.0 / (10000.0 ** (idx / half))
    ang = pos[None, :] * inv[:, None]
    cos = np.cos(ang); sin = np.sin(ang)
    c64 = np.concatenate([cos, cos], 0)
    s64 = np.concatenate([-sin, sin], 0)
    c128 = np.concatenate([c64, c64], 0)
    s128 = np.concatenate([s64, s64], 0)
    return c128, s128


def kernel(L0_attn_x, L0_attn_wq_w, L0_attn_wq_b, L0_attn_wk_w, L0_attn_wk_b,
           L0_attn_wv_w, L0_attn_wv_b, L0_attn_wo_w, L0_attn_wo_b,
           L0_ffn_gate_w, L0_ffn_w1, L0_ffn_b1, L0_ffn_w2, L0_ffn_b2):
    f32c = lambda a: np.ascontiguousarray(np.asarray(a, dtype=np.float32))
    x = f32c(L0_attn_x)
    wq, wk, wv, wo = f32c(L0_attn_wq_w), f32c(L0_attn_wk_w), f32c(L0_attn_wv_w), f32c(L0_attn_wo_w)
    bq, bk, bv, bo = f32c(L0_attn_wq_b), f32c(L0_attn_wk_b), f32c(L0_attn_wv_b), f32c(L0_attn_wo_b)
    gate = f32c(L0_ffn_gate_w)
    w1, b1, w2, b2 = f32c(L0_ffn_w1), f32c(L0_ffn_b1), f32c(L0_ffn_w2), f32c(L0_ffn_b2)

    do = np.arange(D)
    perm = (do - (do % HD)) + ((do % HD) + 32) % HD
    wqr = np.ascontiguousarray(wq[:, perm]); wkr = np.ascontiguousarray(wk[:, perm])
    bqr = bq[perm]; bkr = bk[perm]
    bqkT = np.ascontiguousarray(
        np.concatenate([bq.reshape(DC, 128), bqr.reshape(DC, 128),
                        bk.reshape(DC, 128), bkr.reshape(DC, 128)], 0))

    w1r = np.ascontiguousarray(w1.reshape(NE, D, FC, 128).transpose(0, 2, 1, 3))
    w2r = np.ascontiguousarray(w2.reshape(NE, FF, DC, 128).transpose(0, 2, 1, 3))
    b1r = np.ascontiguousarray(b1.reshape(NE, FC, 128))

    c128, s128 = _rope_tables()
    shared = dict(
        wq=wq, wqr=wqr, wk=wk, wkr=wkr, wv=wv, wo=wo,
        bqkTd=bqkT, bv=bv.reshape(1, D), bo=bo.reshape(1, D),
        gate=gate, w1r=w1r, w2r=w2r, b1=b1r, b2=b2,
        ident=np.eye(128, dtype=np.float32),
        identb=np.eye(128).astype(ml_dtypes.bfloat16))

    in_maps = []
    for c in range(8):
        b, h = c // 2, c % 2
        xb = np.ascontiguousarray(
            np.concatenate([x[b, h * TQ:(h + 1) * TQ], x[b, (1 - h) * TQ:(2 - h) * TQ]], 0))
        tord = np.concatenate([np.arange(h * TQ, (h + 1) * TQ),
                               np.arange((1 - h) * TQ, (2 - h) * TQ)])
        m = dict(shared)
        m["xb"] = xb
        for nm, t64 in (("cosk", c128[:, tord]), ("sink", s128[:, tord])):
            hi = t64.astype(np.float32)
            m[nm] = np.ascontiguousarray(hi)
            m[nm + "l"] = np.ascontiguousarray((t64 - hi).astype(np.float32))
        in_maps.append(m)

    nc = build()
    res = run_bass_kernel_spmd(nc, in_maps, core_ids=list(range(8)))
    out = np.empty((4, T, D), dtype=res.results[0]["out_c"].dtype)
    for c in range(8):
        b, h = c // 2, c % 2
        out[b, h * TQ:(h + 1) * TQ] = res.results[c]["out_c"]
    return out
